# version 10
# baseline (speedup 1.0000x reference)
import contextlib
import numpy as np
import ml_dtypes
import concourse.bass as bass
import concourse.mybir as mybir
from concourse.bass_utils import run_bass_kernel_spmd

F32 = mybir.dt.float32
BF16 = mybir.dt.bfloat16
AF = mybir.ActivationFunctionType
ALU = mybir.AluOpType
AX = mybir.AxisListType

B, T, D, LDIM, H = 8, 4096, 1024, 256, 4096
NT = T // 128
NDC = D // 128
NHC = H // 128
M1 = float(np.float32(1.5 * 2.0 ** 23))
M2 = float(1.5 * 2.0 ** 52)
EPS = 1e-5
ATT_SCALE = 1.0 / 32.0

COMPUTE_ENGINES = ("vector", "scalar", "tensor")
DEBUG = False
PHASES = 'full'
DETECT_RACES = True
A1_PE = True
NT_A2_LIM = None
A2STOP = 'T'
ZN_H1 = False


class Sched:
    ENGINES = ("sync", "gpsimd", "vector", "scalar", "tensor")

    def __init__(self):
        self.ops = []
        self.cnt = {e: 0 for e in self.ENGINES}

    def emit(self, engine, fn, waits=(), inc=0, force_self=False):
        w = []
        for (en, val) in waits:
            if val is None or val <= 0:
                continue
            if en == engine and engine in COMPUTE_ENGINES and not force_self:
                continue
            w.append((en, val))
        self.ops.append((engine, fn, tuple(w), inc))
        if inc:
            self.cnt[engine] += inc
        return dict(self.cnt)

    def now(self, engine):
        return self.cnt[engine]


def build_nc():
    nc = bass.Bass("TRN2", target_bir_lowering=False,
                   dynamic_dma_scratch_size=16384,
                   detect_race_conditions=DETECT_RACES)

    x_h = nc.dram_tensor("x", [T, D], F32, kind="ExternalInput").ap()
    wt_h = nc.dram_tensor("wt", [LDIM, D], F32, kind="ExternalInput").ap()
    wp_h = nc.dram_tensor("wp", [LDIM, D], F32, kind="ExternalInput").ap()
    wg_h = nc.dram_tensor("wg", [D, D], F32, kind="ExternalInput").ap()
    w1_h = nc.dram_tensor("w1", [H, D], F32, kind="ExternalInput").ap()
    w2_h = nc.dram_tensor("w2", [D, H], F32, kind="ExternalInput").ap()
    ln1g_h = nc.dram_tensor("ln1g", [128, D], F32, kind="ExternalInput").ap()
    ln1b_h = nc.dram_tensor("ln1b", [128, D], F32, kind="ExternalInput").ap()
    ln2g_h = nc.dram_tensor("ln2g", [128, H], F32, kind="ExternalInput").ap()
    ln2b_h = nc.dram_tensor("ln2b", [128, H], F32, kind="ExternalInput").ap()
    b1_h = nc.dram_tensor("b1r", [1, H], F32, kind="ExternalInput").ap()
    b2_h = nc.dram_tensor("b2r", [1, D], F32, kind="ExternalInput").ap()
    id_h = nc.dram_tensor("ident", [128, 128], BF16, kind="ExternalInput").ap()
    out_h = nc.dram_tensor("out", [T, D], F32, kind="ExternalOutput").ap()
    xnscr = nc.dram_tensor("xnscr", [T, D], BF16, kind="Internal").ap()
    thscr = nc.dram_tensor("thscr", [2, 128, T], BF16, kind="Internal").ap()
    redscr = nc.dram_tensor("redscr", [128, 1], F32, kind="Internal").ap()

    S = Sched()
    _orig_emit = S.emit
    def _emit(engine, fn, waits=(), inc=0, force_self=False):
        if engine == "tensor" and not A1_PE and getattr(_emit, "_in_a1", False):
            return _orig_emit(engine, lambda e: None, waits, 0, force_self) or dict(S.cnt)
        return _orig_emit(engine, fn, waits, inc, force_self)


    with contextlib.ExitStack() as ctx:
        sbt = lambda nm, shp, dt: ctx.enter_context(nc.sbuf_tensor(nm, shp, dt))
        pst = lambda nm, shp, dt: ctx.enter_context(nc.psum_tensor(nm, shp, dt))

        ps_mm1 = pst("ps_mm1", [128, 2048], F32)
        ps_mm2 = pst("ps_mm2", [128, 1024], F32)
        ps_aux = pst("ps_aux", [128, 512], F32)
        ps_tp = pst("ps_tp", [128, 1024], BF16)

        onesb = sbt("onesb", [128, 1], F32)
        zerob = sbt("zerob", [128, 1], F32)
        m1b = sbt("m1b", [128, 1], F32)
        m2b = sbt("m2b", [128, 1], F32)
        kb = sbt("kb", [128, 1], F32)
        omkb = sbt("omkb", [128, 1], F32)
        m2kb = sbt("m2kb", [128, 1], F32)
        c3b = sbt("c3b", [128, 1], F32)
        onesrowb = sbt("onesrowb", [1, 128], BF16)
        ttmpb = sbt("ttmpb", [128, 1], BF16)
        redsc = sbt("redsc", [1, 4], F32)
        if DEBUG:
            dbg_red1 = sbt("dbg_red1", [1, 4], F32)
            dbg_xn0 = sbt("dbg_xn0", [128, 8], BF16)
            dbg_sc0 = sbt("dbg_sc0", [128, 8], F32)
            dbg_bn0 = sbt("dbg_bn0", [128, 12], F32)
            dbg_ttmp1 = sbt("dbg_ttmp1", [128, 2], F32)
            dbg_part1 = sbt("dbg_part1", [128, 8], F32)
        qw1T = sbt("qw1T", [128, NDC, H], BF16)
        qw2T = sbt("qw2T", [128, NHC, D], BF16)
        ln2gB = sbt("ln2gB", [128, H], BF16)
        ln2bB = sbt("ln2bB", [128, H], BF16)
        b1row = sbt("b1row", [1, H], BF16)
        b2row = sbt("b2row", [1, D], BF16)
        idsb = sbt("idsb", [128, 128], BF16)
        sw1b = sbt("sw1b", [128, 1], F32)
        sw2b = sbt("sw2b", [128, 1], F32)
        sw127b = sbt("sw127b", [128, 1], F32)
        sw2127b = sbt("sw2127b", [128, 1], F32)
        Msb = sbt("Msb", [128, 2, 1024], BF16)
        x_t = sbt("x_t", [128, D], F32)
        out_sb = sbt("out_sb", [128, D], F32)
        xn_t = sbt("xn_t", [128, D], BF16)
        fA = sbt("fA", [128, 512], F32)
        fB = sbt("fB", [128, 512], F32)
        bA = sbt("bA", [128, 512], BF16)
        bB = sbt("bB", [128, 512], BF16)
        bnbuf = sbt("bnbuf", [128, 60], F32)
        mv = sbt("mv", [128, 2], F32)
        partials = sbt("partials", [128, 32], F32)
        v1row = sbt("v1row", [1, 128], BF16)
        v2row = sbt("v2row", [1, 128], BF16)
        tk = {}
        for nm in ("rv", "nm", "rstd", "negmr", "amax", "s1", "sxw", "c1",
                   "v1bf", "mean2", "e2", "msq", "var2", "rv2", "nm2",
                   "rstd2", "negmr2", "amax2", "s2", "sxw2", "c2", "v2bf",
                   "ttmp"):
            tk[nm] = sbt("tk_" + nm, [128, 1],
                         BF16 if nm in ("v1bf", "v2bf") else F32)

        a1stack = ctx.enter_context(contextlib.ExitStack())
        sba = lambda nm, shp, dt: a1stack.enter_context(nc.sbuf_tensor(nm, shp, dt))
        xnT = sba("xnT", [128, D], BF16)
        WtT = sba("WtT", [128, NDC, LDIM], BF16)
        WpT = sba("WpT", [128, NDC, LDIM], BF16)
        WgT = sba("WgT", [128, NDC, D], BF16)
        phi_sb = sba("phi_sb", [128, LDIM], BF16)
        g_sb = sba("g_sb", [128, D], BF16)
        thst = sba("thst", [128, 256], BF16)

        stage_use = {0: [], 1: []}

        for buf, val in ((onesb, 1.0), (zerob, 0.0), (m1b, M1), (m2b, M2),
                         (onesrowb, 1.0), (redsc, 0.0)) + (
                             ((dbg_sc0, 0.0),) if DEBUG else ()):
            S.emit("vector",
                   (lambda b2, v2: lambda e: e.memset(b2[:], v2))(buf, val),
                   inc=1)
        S.emit("vector", lambda e: e.engine_nop() if False else
               e.tensor_copy(out=kb[:], in_=onesb[:]),
               waits=[("vector", S.now("vector"))], inc=1, force_self=True)
        S.emit("vector", lambda e: e.tensor_scalar(
            out=kb[:], in0=kb[:], scalar1=m2b[:], scalar2=m2b[:],
            op0=ALU.add, op1=ALU.subtract),
            waits=[("vector", S.now("vector"))], inc=1, force_self=True)
        S.emit("vector", lambda e: e.tensor_scalar(
            out=omkb[:], in0=kb[:], scalar1=1.0, scalar2=None, op0=ALU.is_lt),
            waits=[("vector", S.now("vector"))], inc=1, force_self=True)
        S.emit("vector", lambda e: e.tensor_scalar(
            out=m2kb[:], in0=kb[:], scalar1=m2b[:], scalar2=None, op0=ALU.mult),
            waits=[("vector", S.now("vector"))], inc=1, force_self=True)
        S.emit("vector", lambda e: e.tensor_scalar(
            out=c3b[:], in0=m2kb[:], scalar1=m1b[:], scalar2=None,
            op0=ALU.subtract),
            waits=[("vector", S.now("vector"))], inc=1, force_self=True)
        c_omk = S.now("vector")

        S.emit("scalar", lambda e: e.preload_act_table(AF.Sqrt), inc=1)
        S.emit("sync", lambda e: e.dma_start(out=idsb[:], in_=id_h), inc=16)
        c_ident = S.now("sync")
        for dst, src in ((ln2gB, ln2g_h),
                         (ln2bB, ln2b_h), (b1row, b1_h), (b2row, b2_h)):
            S.emit("gpsimd",
                   (lambda d, s: lambda e: e.dma_start(out=d[:], in_=s))(dst, src),
                   inc=16)
        c_smallg = S.now("gpsimd")

        def prep_quant_weight(srcs, swb, dsts):
            nch = len(srcs)
            for hc in range(nch):
                stage = x_t if hc % 2 == 0 else out_sb
                skey = hc % 2
                wv = S.emit("sync", (lambda s, src: lambda e: e.dma_start(
                    out=s[:], in_=src))(stage, srcs[hc]),
                    waits=list(stage_use[skey]), inc=16)
                stage_use[skey] = []
                cv = S.emit("scalar", (lambda s, h: lambda e: e.activation(
                    out=xn_t[:], in_=s[:], func=AF.Abs,
                    bias=zerob[:], scale=onesb[:]))(stage, hc),
                    waits=[("sync", wv["sync"]), ("vector", c_omk),
                           ("vector", S.now("vector"))], inc=1)
                stage_use[skey].append(("scalar", cv["scalar"]))
                S.emit("vector", (lambda h: lambda e: e.reduce_sum(
                    out=partials[:, h:h + 1], in_=xn_t[:], axis=AX.X))(hc),
                    waits=[("scalar", cv["scalar"])], inc=1)
                stage_use[skey].append(("vector", S.now("vector")))
            S.emit("vector", lambda e: e.reduce_sum(
                out=tk["ttmp"][:], in_=partials[:], axis=AX.X),
                waits=[("vector", S.now("vector"))], inc=1, force_self=True)
            S.emit("vector", lambda e: e.tensor_copy(
                out=ttmpb[:], in_=tk["ttmp"][:]),
                waits=[("vector", S.now("vector"))], inc=1, force_self=True)
            S.emit("vector", lambda e: e.tensor_copy(
                out=xn_t[:, 0:1], in_=ttmpb[:]),
                waits=[("vector", S.now("vector"))], inc=1, force_self=True)
            S.emit("tensor", lambda e: e.transpose(
                out=ps_tp[:, 0:128], in_=xn_t[:, 0:128], identity=idsb[:]),
                waits=[("vector", S.now("vector")), ("sync", c_ident)], inc=1)
            S.emit("vector", lambda e: e.reduce_sum(
                out=redsc[0:1, 0:1], in_=ps_tp[0:1, 0:128], axis=AX.X),
                waits=[("tensor", S.now("tensor"))], inc=1)
            S.emit("vector", lambda e: e.tensor_scalar(
                out=redsc[0:1, 1:2], in0=redsc[0:1, 0:1],
                scalar1=1.0 / float(32 * 128 * 1024), scalar2=1e-5,
                op0=ALU.mult, op1=ALU.max),
                waits=[("vector", S.now("vector"))], inc=1, force_self=True)
            S.emit("vector", lambda e: e.reciprocal(
                out=redsc[0:1, 2:3], in_=redsc[0:1, 1:2]),
                waits=[("vector", S.now("vector"))], inc=1, force_self=True)
            S.emit("vector", lambda e: e.tensor_scalar(
                out=xn_t[0:1, 0:128], in0=onesrowb[:], scalar1=redsc[0:1, 2:3],
                scalar2=None, op0=ALU.mult),
                waits=[("vector", S.now("vector"))], inc=1, force_self=True)
            S.emit("tensor", lambda e: e.transpose(
                out=ps_tp[:, 0:128], in_=xn_t[:, 0:128], identity=idsb[:]),
                waits=[("vector", S.now("vector"))], inc=1)
            S.emit("vector", (lambda w: lambda e: e.tensor_copy(
                out=w[:], in_=ps_tp[:, 0:1]))(swb),
                waits=[("tensor", S.now("tensor"))], inc=1)
            if DEBUG and swb is sw1b:
                def f_snap(e):
                    e.tensor_copy(out=dbg_red1[:], in_=redsc[:])
                    e.tensor_copy(out=dbg_ttmp1[:, 0:1], in_=tk["ttmp"][:])
                    e.tensor_copy(out=dbg_ttmp1[:, 1:2], in_=ps_tp[:, 0:1])
                    return e.tensor_copy(out=dbg_part1[:], in_=partials[:, 0:8])
                S.emit("vector", f_snap,
                       waits=[("vector", S.now("vector"))], inc=1,
                       force_self=True)
            c_swb = S.now("vector")

            for hc in range(nch):
                stage = x_t if hc % 2 == 0 else out_sb
                skey = hc % 2
                wv = S.emit("sync", (lambda s, src: lambda e: e.dma_start(
                    out=s[:], in_=src))(stage, srcs[hc]),
                    waits=list(stage_use[skey]) + [("vector", c_swb)], inc=16)
                stage_use[skey] = []

                def f_q(s, w):
                    def f(e):
                        r = None
                        for half in range(2):
                            sl = slice(half * 512, half * 512 + 512)
                            e.tensor_scalar(out=fA[:], in0=s[:, sl],
                                            scalar1=w[:], scalar2=m1b[:],
                                            op0=ALU.mult, op1=ALU.add)
                            e.tensor_scalar(out=fB[:], in0=fA[:],
                                            scalar1=c3b[:], scalar2=m2kb[:],
                                            op0=ALU.add, op1=ALU.subtract)
                            r = e.tensor_scalar(out=xn_t[:, sl], in0=fB[:],
                                                scalar1=-1.0, scalar2=1.0,
                                                op0=ALU.max, op1=ALU.min)
                        return r
                    return f
                cv = S.emit("vector", f_q(stage, swb),
                            waits=[("sync", wv["sync"]),
                                   ("vector", S.now("vector"))],
                            inc=1, force_self=True)
                stage_use[skey].append(("vector", cv["vector"]))
                c_q = S.now("vector")

                def f_tp(e):
                    r = None
                    for dc in range(8):
                        r = e.transpose(out=ps_tp[:, dc * 128:(dc + 1) * 128],
                                        in_=xn_t[:, dc * 128:(dc + 1) * 128],
                                        identity=idsb[:])
                    return r
                S.emit("tensor", f_tp,
                       waits=[("vector", c_q), ("sync", c_ident)], inc=1)
                S.emit("vector", (lambda dst: lambda e: e.tensor_copy(
                    out=dst, in_=ps_tp[:]))(dsts[hc]),
                    waits=[("tensor", S.now("tensor"))], inc=1)
            return S.now("vector")

        w1srcs = [w1_h[hc * 128:(hc + 1) * 128, :] for hc in range(32)]
        w1dsts = [qw1T[:, 0:NDC, hc * 128:(hc + 1) * 128] for hc in range(32)]
        prep_quant_weight(w1srcs, sw1b, w1dsts)
        w2srcs, w2dsts = [], []
        for rc in range(8):
            for cg in range(4):
                w2srcs.append(w2_h[rc * 128:(rc + 1) * 128,
                                   cg * 1024:(cg + 1) * 1024])
                w2dsts.append(qw2T[:, cg * 8:cg * 8 + 8, rc * 128:(rc + 1) * 128])
        c_wq = prep_quant_weight(w2srcs, sw2b, w2dsts)
        def f_sw127(e):
            e.tensor_scalar(out=sw127b[:], in0=sw1b[:], scalar1=127.0,
                            scalar2=None, op0=ALU.mult)
            return e.tensor_scalar(out=sw2127b[:], in0=sw2b[:], scalar1=127.0,
                                   scalar2=None, op0=ALU.mult)
        S.emit("vector", f_sw127, waits=[("vector", S.now("vector"))],
               inc=1, force_self=True)
        c_wq = S.now("vector")

        c_xnt_use = []

        def prep_attn_weight(w_hbm, nlc, dstT):
            nonlocal c_xnt_use
            for lc in range(nlc):
                wv = S.emit("gpsimd", (lambda l, w: lambda e: e.dma_start(
                    out=xn_t[:], in_=w[l * 128:(l + 1) * 128, :]))(lc, w_hbm),
                    waits=[("vector", S.now("vector"))], inc=16)

                def f_tp(e):
                    r = None
                    for dc in range(NDC):
                        r = e.transpose(out=ps_tp[:, dc * 128:(dc + 1) * 128],
                                        in_=xn_t[:, dc * 128:(dc + 1) * 128],
                                        identity=idsb[:])
                    return r
                S.emit("tensor", f_tp,
                       waits=[("gpsimd", wv["gpsimd"]), ("sync", c_ident)],
                       inc=1)
                S.emit("vector", (lambda l, d: lambda e: e.tensor_copy(
                    out=d[:, 0:NDC, l * 128:(l + 1) * 128],
                    in_=ps_tp[:]))(lc, dstT),
                    waits=[("tensor", S.now("tensor"))], inc=1)
            return S.now("vector")

        prep_attn_weight(wt_h, 2, WtT)
        prep_attn_weight(wp_h, 2, WpT)
        c_wattn = prep_attn_weight(wg_h, 8, WgT)

        c_xapply = {}
        c_xnw = {}
        c_thw = {}
        c_xnT_tp = {}
        NT_A1 = 0 if PHASES == 'pre' else NT
        for i in range(NT_A1):
            xwaits = [("scalar", c_xapply.get(i - 1, 0))]
            if i == 0:
                xwaits += [("vector", c_wattn)]
            wv_x = S.emit("sync", (lambda ii: lambda e: e.dma_start(
                out=x_t[:], in_=x_h[ii * 128:(ii + 1) * 128, :]))(i),
                waits=xwaits, inc=16)

            def f_bn_a(e):
                e.bn_stats(out=bnbuf[:, 0:6], in_=x_t[:, 0:512])
                return e.bn_stats(out=bnbuf[:, 6:12], in_=x_t[:, 512:1024])
            def f_bn_b(e):
                return e.bn_aggr(out=mv[:], in_=bnbuf[:, 0:12])
            def f_bn_c(e):
                e.tensor_scalar(out=tk["ttmp"][:], in0=mv[:, 1:2], scalar1=EPS,
                                scalar2=None, op0=ALU.add)
                return e.tensor_scalar(out=tk["nm"][:], in0=mv[:, 0:1],
                                       scalar1=-1.0, scalar2=None, op0=ALU.mult)
            bnw = [("sync", wv_x["sync"])]
            if i == 0:
                bnw += [("vector", c_wattn)]
            S.emit("vector", f_bn_a, waits=bnw, inc=1)
            S.emit("vector", f_bn_b, waits=[("vector", S.now("vector"))],
                   inc=1, force_self=True)
            S.emit("vector", f_bn_c, waits=[("vector", S.now("vector"))],
                   inc=1, force_self=True)
            S.emit("vector", lambda e: e.reciprocal(
                out=tk["rv"][:], in_=tk["ttmp"][:]),
                waits=[("vector", S.now("vector"))], inc=1, force_self=True)
            c_stats = S.now("vector")
            S.emit("scalar", lambda e: e.activation(
                out=tk["rstd"][:], in_=tk["rv"][:], func=AF.Sqrt,
                bias=zerob[:], scale=onesb[:]),
                waits=[("vector", c_stats)], inc=1)
            S.emit("scalar", lambda e: e.activation(
                out=tk["negmr"][:], in_=tk["nm"][:], func=AF.Identity,
                bias=zerob[:], scale=tk["rstd"][:]),
                waits=[("scalar", S.now("scalar"))], inc=1, force_self=True)
            S.emit("scalar", lambda e: e.activation(
                out=xn_t[:], in_=x_t[:], func=AF.Identity,
                bias=tk["negmr"][:], scale=tk["rstd"][:]),
                waits=[("scalar", S.now("scalar")),
                       ("gpsimd", c_xnw.get(i - 1, c_smallg)),
                       ("tensor", c_xnT_tp.get(i - 1, 0))],
                inc=1, force_self=True)
            c_xapply[i] = S.now("scalar")
            if DEBUG and i == 0:
                def f_snap0(e):
                    e.tensor_copy(out=dbg_xn0[:], in_=xn_t[:, 0:8])
                    e.tensor_copy(out=dbg_sc0[:, 0:1], in_=tk["rstd"][:])
                    e.tensor_copy(out=dbg_sc0[:, 1:2], in_=tk["negmr"][:])
                    e.tensor_copy(out=dbg_sc0[:, 2:3], in_=tk["rv"][:])
                    e.tensor_copy(out=dbg_sc0[:, 3:4], in_=tk["nm"][:])
                    e.tensor_copy(out=dbg_sc0[:, 4:5], in_=tk["ttmp"][:])
                    e.tensor_copy(out=dbg_sc0[:, 5:7], in_=mv[:])
                    return e.tensor_copy(out=dbg_bn0[:], in_=bnbuf[:, 0:12])
                S.emit("vector", f_snap0,
                       waits=[("scalar", c_xapply[i])], inc=1)
            wv = S.emit("gpsimd", (lambda ii: lambda e: e.dma_start(
                out=xnscr[ii * 128:(ii + 1) * 128, :], in_=xn_t[:]))(i),
                waits=[("scalar", c_xapply[i])], inc=16)
            c_xnw[i] = wv["gpsimd"]

            def f_xnT(e):
                r = None
                for dc in range(NDC):
                    r = e.transpose(out=ps_tp[:, dc * 128:(dc + 1) * 128],
                                    in_=xn_t[:, dc * 128:(dc + 1) * 128],
                                    identity=idsb[:])
                return r
            cv = S.emit("tensor", f_xnT,
                        waits=[("scalar", c_xapply[i]),
                               ("vector", S.now("vector"))], inc=1)
            c_xnT_tp[i] = cv["tensor"]
            S.emit("vector", lambda e: e.tensor_copy(out=xnT[:], in_=ps_tp[:]),
                   waits=[("tensor", S.now("tensor"))], inc=1)
            c_xnT = S.now("vector")

            def f_phi(e):
                r = None
                for c in range(NDC):
                    r = e.matmul(out=ps_aux[:, 0:LDIM],
                                 lhsT=xnT[:, c * 128:(c + 1) * 128],
                                 rhs=WpT[:, c, :],
                                 start=(c == 0), stop=(c == NDC - 1))
                return r
            S.emit("tensor", f_phi, waits=[("vector", c_xnT)], inc=1)
            c_phimm = S.now("tensor")

            def f_g(e):
                r = None
                for c in range(NDC):
                    for n in range(2):
                        r = e.matmul(out=ps_mm2[:, n * 512:(n + 1) * 512],
                                     lhsT=xnT[:, c * 128:(c + 1) * 128],
                                     rhs=WgT[:, c, n * 512:(n + 1) * 512],
                                     start=(c == 0), stop=(c == NDC - 1))
                return r
            S.emit("tensor", f_g, inc=1)
            c_gmm = S.now("tensor")
            S.emit("vector", lambda e: e.tensor_copy(out=phi_sb[:],
                                                     in_=ps_aux[:, 0:LDIM]),
                   waits=[("tensor", c_phimm)], inc=1)
            c_phiev = S.now("vector")
            S.emit("vector", lambda e: e.tensor_copy(out=g_sb[:], in_=ps_mm2[:]),
                   waits=[("tensor", c_gmm)], inc=1)
            c_gev = S.now("vector")

            for lc in range(2):
                def f_th(l):
                    def f(e):
                        r = None
                        for c in range(NDC):
                            r = e.matmul(out=ps_aux[:, 0:128],
                                         lhsT=WtT[:, c, l * 128:(l + 1) * 128],
                                         rhs=xnT[:, c * 128:(c + 1) * 128],
                                         start=(c == 0), stop=(c == NDC - 1))
                        return r
                    return f
                S.emit("tensor", f_th(lc),
                       waits=[("vector",
                               c_phiev if lc == 0 else S.now("vector"))], inc=1)
                S.emit("vector", (lambda l: lambda e: e.tensor_copy(
                    out=thst[:, l * 128:(l + 1) * 128],
                    in_=ps_aux[:, 0:128]))(lc),
                    waits=[("tensor", S.now("tensor")),
                           ("gpsimd", c_thw.get(i - 1, c_smallg) if lc == 0
                            else 0)], inc=1)
            c_thev = S.now("vector")

            S.emit("gpsimd", (lambda ii: lambda e: e.dma_start(
                out=thscr[0, :, ii * 128:(ii + 1) * 128],
                in_=thst[:, 0:128]))(i),
                waits=[("vector", c_thev)], inc=16)
            wv = S.emit("gpsimd", (lambda ii: lambda e: e.dma_start(
                out=thscr[1, :, ii * 128:(ii + 1) * 128],
                in_=thst[:, 128:256]))(i),
                waits=[("vector", c_thev)], inc=16)
            c_thw[i] = wv["gpsimd"]

            def f_M(ii):
                def f(e):
                    r = None
                    for lc in range(2):
                        for n in range(2):
                            o = ps_mm1[:, lc * 1024 + n * 512:
                                       lc * 1024 + n * 512 + 512]
                            r = e.matmul(out=o,
                                         lhsT=phi_sb[:, lc * 128:(lc + 1) * 128],
                                         rhs=g_sb[:, n * 512:(n + 1) * 512],
                                         start=(ii == 0), stop=(ii == NT - 1))
                    return r
                return f
            S.emit("tensor", f_M(i), waits=[("vector", c_gev)], inc=1)
        c_Mdone = S.now("tensor")
        c_A1_gp = S.now("gpsimd")

        def f_Mev(e):
            e.tensor_scalar(out=Msb[:, 0, :], in0=ps_mm1[:, 0:1024],
                            scalar1=ATT_SCALE, scalar2=None, op0=ALU.mult)
            return e.tensor_scalar(out=Msb[:, 1, :], in0=ps_mm1[:, 1024:2048],
                                   scalar1=ATT_SCALE, scalar2=None, op0=ALU.mult)
        if PHASES != 'pre':
            S.emit("vector", f_Mev, waits=[("tensor", c_Mdone)], inc=1)
        c_Msb = S.now("vector")
        c_A1_scalar = S.now("scalar")

        a1stack.close()
        qx = sbt("qx", [128, D], BF16)
        qxT = sbt("qxT", [128, D], BF16)
        u_sb = sbt("u_sb", [128, H], BF16)
        H1 = sbt("H1", [128, H], BF16)
        qh_sb = sbt("qh_sb", [128, 1024], BF16)
        qhT0 = sbt("qhT0", [128, 1024], BF16)
        qhT1 = sbt("qhT1", [128, 1024], BF16)
        tht0 = sbt("tht0", [128, 256], BF16)
        tht1 = sbt("tht1", [128, 256], BF16)
        ln1gB = sbt("ln1gB", [128, D], BF16)
        ln1bB = sbt("ln1bB", [128, D], BF16)
        for dst, src in ((ln1gB, ln1g_h), (ln1bB, ln1b_h)):
            S.emit("gpsimd",
                   (lambda d, s: lambda e: e.dma_start(out=d[:], in_=s))(dst, src),
                   waits=[("tensor", c_Mdone)], inc=16)

        NT_A2 = NT if PHASES == 'full' else 0
        if NT_A2_LIM is not None:
            NT_A2 = min(NT_A2, NT_A2_LIM)
        c_gelu_q = {}
        c_zn1_t, c_o1_t, c_fin_t, c_att_t = {}, {}, {}, {}
        c_outdma, c_xdma, c_thdma, c_xndma = {}, {}, {}, {}
        c_qh3_t, c_mm1q3, c_mm2q3 = {}, {}, {}
        c_v1T_t, c_v2T_t, c_qxTtp, c_rne1_t, c_c1_t = {}, {}, {}, {}, {}
        c_qhev3 = {}

        def f_att(d):
            def f(e):
                r = None
                for lc in range(2):
                    for n in range(2):
                        r = e.matmul(out=ps_mm2[:, n * 512:(n + 1) * 512],
                                     lhsT=d[:, lc * 128:(lc + 1) * 128],
                                     rhs=Msb[:, lc, n * 512:(n + 1) * 512],
                                     start=(lc == 0), stop=(lc == 1))
                return r
            return f

        def emit_tail(j):
            S.emit("vector", lambda e: e.scalar_tensor_tensor(
                out=out_sb[:], in0=ps_mm2[:], scalar=tk["c2"][:], in1=x_t[:],
                op0=ALU.mult, op1=ALU.add),
                waits=[("tensor", c_mm2q3[j]), ("sync", c_xdma[j]),
                       ("sync", c_outdma.get(j - 1, 0)),
                       ("vector", S.now("vector"))], inc=1, force_self=True)
            c_o1_t[j] = S.now("vector")
            tht_j = tht0 if j % 2 == 0 else tht1
            S.emit("tensor", f_att(tht_j),
                   waits=[("vector", max(c_o1_t[j], c_Msb)),
                          ("gpsimd", c_thdma[j])], inc=1)
            c_att_t[j] = S.now("tensor")
            S.emit("vector", lambda e: e.scalar_tensor_tensor(
                out=out_sb[:], in0=ps_mm2[:], scalar=1.0, in1=out_sb[:],
                op0=ALU.mult, op1=ALU.add),
                waits=[("tensor", c_att_t[j])], inc=1)
            c_fin_t[j] = S.now("vector")
            S.emit("sync", (lambda jj: lambda e: e.dma_start(
                out=out_h[jj * 128:(jj + 1) * 128, :], in_=out_sb[:]))(j),
                waits=[("vector", c_fin_t[j])], inc=16)
            c_outdma[j] = S.now("sync")

        def emit_xn_dma(j, extra):
            wv = S.emit("gpsimd", (lambda jj: lambda e: e.dma_start(
                out=xn_t[:], in_=xnscr[jj * 128:(jj + 1) * 128, :]))(j),
                waits=[("gpsimd", c_A1_gp)] + extra, inc=16)
            c_xndma[j] = wv["gpsimd"]

        def emit_tht_dma(j, extra):
            d = tht0 if j % 2 == 0 else tht1
            S.emit("gpsimd", (lambda jj, dd: lambda e: e.dma_start(
                out=dd[:, 0:128],
                in_=thscr[0, :, jj * 128:(jj + 1) * 128]))(j, d),
                waits=[("gpsimd", c_A1_gp)] + extra, inc=16)
            wv = S.emit("gpsimd", (lambda jj, dd: lambda e: e.dma_start(
                out=dd[:, 128:256],
                in_=thscr[1, :, jj * 128:(jj + 1) * 128]))(j, d),
                waits=[], inc=16)
            c_thdma[j] = wv["gpsimd"]

        def emit_L(j):
            def f_bn1_a(e):
                e.bn_stats(out=bnbuf[:, 48:54], in_=xn_t[:, 0:512])
                return e.bn_stats(out=bnbuf[:, 54:60], in_=xn_t[:, 512:1024])
            def f_bn1_b(e):
                return e.bn_aggr(out=mv[:], in_=bnbuf[:, 48:60])
            def f_bn1_c(e):
                e.tensor_scalar(out=tk["ttmp"][:], in0=mv[:, 1:2], scalar1=EPS,
                                scalar2=None, op0=ALU.add)
                return e.tensor_scalar(out=tk["nm"][:], in0=mv[:, 0:1],
                                       scalar1=-1.0, scalar2=None, op0=ALU.mult)
            S.emit("vector", f_bn1_a, waits=[("gpsimd", c_xndma[j])], inc=1)
            if A2STOP == 'BN0':
                return
            S.emit("vector", f_bn1_b, waits=[("vector", S.now("vector"))],
                   inc=1, force_self=True)
            S.emit("vector", f_bn1_c, waits=[("vector", S.now("vector"))],
                   inc=1, force_self=True)
            S.emit("vector", lambda e: e.reciprocal(
                out=tk["rv"][:], in_=tk["ttmp"][:]),
                waits=[("vector", S.now("vector"))], inc=1, force_self=True)
            c_st1 = S.now("vector")
            if A2STOP == 'RV':
                return
            S.emit("scalar", lambda e: e.preload_act_table(AF.Sqrt), inc=0)
            if A2STOP == 'PRE':
                return
            S.emit("scalar", lambda e: e.activation(
                out=tk["rstd"][:], in_=tk["rv"][:], func=AF.Sqrt,
                bias=zerob[:], scale=onesb[:]),
                waits=[("vector", c_st1)], inc=1)
            if A2STOP == 'RSTD':
                return
            S.emit("scalar", lambda e: e.activation(
                out=tk["negmr"][:], in_=tk["nm"][:], func=AF.Identity,
                bias=zerob[:], scale=tk["rstd"][:]),
                waits=[("scalar", S.now("scalar"))], inc=1, force_self=True)
            if A2STOP == 'NEGMR':
                return
            ZT = H1 if ZN_H1 else qx
            S.emit("scalar", (lambda zt: lambda e: e.activation(
                out=zt[:, 0:D], in_=xn_t[:], func=AF.Identity,
                bias=tk["negmr"][:], scale=tk["rstd"][:]))(ZT),
                waits=[("scalar", S.now("scalar")),
                       ("tensor", c_qxTtp.get(j - 1, 0))],
                inc=1, force_self=True)
            c_zn1_t[j] = S.now("scalar")
            if A2STOP == 'ZN':
                return

            def f_y1_a(e):
                for hh in range(2):
                    sl = slice(hh * 512, hh * 512 + 512)
                    e.tensor_tensor(out=bA[:], in0=qx[:, sl], in1=ln1gB[:, sl],
                                    op=ALU.mult)
                    e.tensor_tensor(out=qx[:, sl], in0=bA[:], in1=ln1bB[:, sl],
                                    op=ALU.add)
                return e.tensor_reduce(out=tk["amax"][:], in_=qx[:],
                                       axis=AX.X, op=ALU.max,
                                       apply_absolute_value=True)
            def f_y1_b(e):
                return e.tensor_scalar(out=tk["amax"][:], in0=tk["amax"][:],
                                       scalar1=1e-5, scalar2=None, op0=ALU.max)
            def f_y1_b2(e):
                return e.reciprocal(out=tk["s1"][:], in_=tk["amax"][:])
            def f_y1_c(e):
                e.tensor_tensor(out=tk["sxw"][:], in0=tk["s1"][:],
                                in1=sw127b[:], op=ALU.mult)
                return e.tensor_scalar(out=tk["s1"][:], in0=tk["s1"][:],
                                       scalar1=127.0, scalar2=None,
                                       op0=ALU.mult)
            S.emit("vector", f_y1_a, waits=[("scalar", c_zn1_t[j])], inc=1)
            if A2STOP == 'Y1A':
                return
            S.emit("vector", f_y1_b, waits=[("vector", S.now("vector"))],
                   inc=1, force_self=True)
            S.emit("vector", f_y1_b2, waits=[("vector", S.now("vector"))],
                   inc=1, force_self=True)
            S.emit("vector", f_y1_c, waits=[("vector", S.now("vector"))],
                   inc=1, force_self=True)
            S.emit("vector", lambda e: e.tensor_copy(
                out=tk["v1bf"][:], in_=tk["sxw"][:]),
                waits=[("vector", S.now("vector")),
                       ("tensor", c_v1T_t.get(j - 1, 0))],
                inc=1, force_self=True)

            def f_rne1(e):
                r = None
                for hh in range(2):
                    sl = slice(hh * 512, hh * 512 + 512)
                    e.tensor_scalar(out=fA[:], in0=qx[:, sl],
                                    scalar1=tk["s1"][:], scalar2=m1b[:],
                                    op0=ALU.mult, op1=ALU.add)
                    r = e.tensor_scalar(out=qx[:, sl], in0=fA[:],
                                        scalar1=c3b[:], scalar2=m2kb[:],
                                        op0=ALU.add, op1=ALU.subtract)
                return r
            S.emit("vector", f_rne1, waits=[("vector", S.now("vector"))],
                   inc=1, force_self=True)
            c_rne1_t[j] = S.now("vector")
            S.emit("vector", lambda e: e.reciprocal(
                out=tk["c1"][:], in_=tk["sxw"][:]),
                waits=[("vector", S.now("vector")),
                       ("scalar", c_gelu_q.get((j - 1, 3), 0))],
                inc=1, force_self=True)
            c_c1_t[j] = S.now("vector")

        if NT_A2:
            emit_xn_dma(0, [("tensor", c_Mdone), ("scalar", c_A1_scalar),
                            ("vector", c_Msb)])
            emit_tht_dma(0, [("tensor", c_Mdone)])
            if A2STOP != 'XN':
                emit_L(0)

        for i in range(NT_A2):
            tht = tht0 if i % 2 == 0 else tht1
            if A2STOP in ('L', 'XN', 'BN0', 'ZN', 'Y1A', 'RV', 'PRE', 'RSTD', 'NEGMR'):
                break

            def f_qxT(e):
                r = None
                for dc in range(NDC):
                    r = e.transpose(out=ps_tp[:, dc * 128:(dc + 1) * 128],
                                    in_=qx[:, dc * 128:(dc + 1) * 128],
                                    identity=idsb[:])
                return r
            cv = S.emit("tensor", f_qxT,
                        waits=[("vector", c_rne1_t[i]),
                               ("scalar", c_qhev3.get(i - 1, 0))], inc=1)
            c_qxTtp[i] = cv["tensor"]
            cv = S.emit("scalar", lambda e: e.activation(
                out=qxT[:], in_=ps_tp[:], func=AF.Identity,
                bias=zerob[:], scale=onesb[:]),
                waits=[("tensor", S.now("tensor")),
                       ("tensor", c_mm1q3.get(i - 1, 0))], inc=1)
            c_qxev = cv["scalar"]
            cv = S.emit("tensor", lambda e: e.transpose(
                out=ps_tp[0:1, 0:128], in_=tk["v1bf"][:], identity=idsb[:]),
                waits=[("scalar", c_qxev)], inc=1)
            c_v1T_t[i] = cv["tensor"]
            S.emit("vector", lambda e: e.tensor_copy(out=v1row[:],
                                                     in_=ps_tp[0:1, 0:128]),
                   waits=[("tensor", S.now("tensor")),
                          ("tensor", c_mm1q3.get(i - 1, 0))], inc=1)
            c_v1r = S.now("vector")

            for q in range(4):
                def f_mm1(qq):
                    def f(e):
                        r = None
                        base = (qq % 2) * 1024
                        for n in range(2):
                            o = ps_mm1[:, base + n * 512: base + n * 512 + 512]
                            for c in range(NDC):
                                e.matmul(out=o,
                                         lhsT=qxT[:, c * 128:(c + 1) * 128],
                                         rhs=qw1T[:, c, qq * 1024 + n * 512:
                                                  qq * 1024 + n * 512 + 512],
                                         start=(c == 0), stop=False)
                            r = e.matmul(out=o, lhsT=v1row[:],
                                         rhs=b1row[0:1, qq * 1024 + n * 512:
                                                   qq * 1024 + n * 512 + 512],
                                         start=False, stop=True)
                        return r
                    return f
                S.emit("tensor", f_mm1(q),
                       waits=[("vector", max(c_v1r, c_wq)),
                              ("scalar", c_qxev),
                              ("scalar", c_gelu_q.get(
                                  (i, q - 2), c_gelu_q.get((i - 1, q + 2), 0)))],
                       inc=1)
                c_mm1q = S.now("tensor")
                if q == 3:
                    c_mm1q3[i] = c_mm1q

                def f_gelu(qq):
                    def f(e):
                        base = (qq % 2) * 1024
                        return e.activation(
                            out=u_sb[:, qq * 1024:(qq + 1) * 1024],
                            in_=ps_mm1[:, base:base + 1024],
                            func=AF.Gelu, bias=zerob[:], scale=tk["c1"][:])
                    return f
                if q == 0:
                    S.emit("scalar", lambda e: e.preload_act_table(AF.Gelu),
                           inc=0)
                cv = S.emit("scalar", f_gelu(q),
                            waits=[("tensor", c_mm1q),
                                   ("vector", c_c1_t[i])], inc=1)
                c_gelu_q[(i, q)] = cv["scalar"]

            if A2STOP == 'A':
                break
            if i > 0:
                emit_tail(i - 1)

            if i + 1 < NT_A2:
                emit_xn_dma(i + 1, [("scalar", c_zn1_t[i])])
                emit_tht_dma(i + 1, [("tensor", c_att_t.get(i - 1, 0))])
            xw = [("vector", c_o1_t.get(i - 1, 0))]
            if i == 0:
                xw += [("scalar", c_A1_scalar), ("vector", c_Msb)]
            wv = S.emit("sync", (lambda ii: lambda e: e.dma_start(
                out=x_t[:], in_=x_h[ii * 128:(ii + 1) * 128, :]))(i),
                waits=xw, inc=16)
            c_xdma[i] = wv["sync"]

            if i + 1 < NT_A2:
                emit_L(i + 1)

            def f_bn2(qq):
                def f(e):
                    e.bn_stats(out=bnbuf[:, qq * 12:qq * 12 + 6],
                               in_=u_sb[:, qq * 1024:qq * 1024 + 512])
                    return e.bn_stats(
                        out=bnbuf[:, qq * 12 + 6:qq * 12 + 12],
                        in_=u_sb[:, qq * 1024 + 512:qq * 1024 + 1024])
                return f
            for q in range(4):
                S.emit("vector", f_bn2(q),
                       waits=[("scalar", c_gelu_q[(i, q)])], inc=1)
            S.emit("vector", lambda e: e.bn_aggr(out=mv[:], in_=bnbuf[:, 0:48]),
                   waits=[("vector", S.now("vector"))], inc=1, force_self=True)
            def f_st2_b(e):
                e.tensor_copy(out=tk["mean2"][:], in_=mv[:, 0:1])
                e.tensor_copy(out=tk["var2"][:], in_=mv[:, 1:2])
                e.tensor_scalar(out=tk["ttmp"][:], in0=mv[:, 1:2],
                                scalar1=EPS, scalar2=None, op0=ALU.add)
                return e.tensor_scalar(out=tk["nm2"][:], in0=mv[:, 0:1],
                                       scalar1=-1.0, scalar2=None, op0=ALU.mult)
            S.emit("vector", f_st2_b, waits=[("vector", S.now("vector"))],
                   inc=1, force_self=True)
            S.emit("vector", lambda e: e.reciprocal(
                out=tk["rv2"][:], in_=tk["ttmp"][:]),
                waits=[("vector", S.now("vector"))], inc=1, force_self=True)
            c_st2 = S.now("vector")
            if i + 1 >= NT_A2:
                S.emit("scalar", lambda e: e.preload_act_table(AF.Sqrt), inc=0)
            S.emit("scalar", lambda e: e.activation(
                out=tk["rstd2"][:], in_=tk["rv2"][:], func=AF.Sqrt,
                bias=zerob[:], scale=onesb[:]),
                waits=[("vector", c_st2)], inc=1)
            S.emit("scalar", lambda e: e.activation(
                out=tk["negmr2"][:], in_=tk["nm2"][:], func=AF.Identity,
                bias=zerob[:], scale=tk["rstd2"][:]),
                waits=[("scalar", S.now("scalar"))], inc=1, force_self=True)
            S.emit("scalar", lambda e: e.activation(
                out=H1[:], in_=u_sb[:], func=AF.Identity,
                bias=tk["negmr2"][:], scale=tk["rstd2"][:]),
                waits=[("scalar", S.now("scalar"))], inc=1, force_self=True)
            c_zn2 = S.now("scalar")

            def f_y2_a(e):
                for hh in range(8):
                    sl = slice(hh * 512, hh * 512 + 512)
                    e.tensor_tensor(out=bA[:], in0=H1[:, sl], in1=ln2gB[:, sl],
                                    op=ALU.mult)
                    e.tensor_tensor(out=H1[:, sl], in0=bA[:], in1=ln2bB[:, sl],
                                    op=ALU.add)
                return e.tensor_reduce(out=tk["amax2"][:], in_=H1[:],
                                       axis=AX.X, op=ALU.max,
                                       apply_absolute_value=True)
            def f_y2_b(e):
                return e.tensor_scalar(out=tk["amax2"][:], in0=tk["amax2"][:],
                                       scalar1=1e-5, scalar2=None, op0=ALU.max)
            def f_y2_b2(e):
                return e.reciprocal(out=tk["s2"][:], in_=tk["amax2"][:])
            def f_y2_c(e):
                e.tensor_tensor(out=tk["sxw2"][:], in0=tk["s2"][:],
                                in1=sw2127b[:], op=ALU.mult)
                return e.tensor_scalar(out=tk["s2"][:], in0=tk["s2"][:],
                                       scalar1=127.0, scalar2=None,
                                       op0=ALU.mult)
            S.emit("vector", f_y2_a, waits=[("scalar", c_zn2)], inc=1)
            S.emit("vector", f_y2_b, waits=[("vector", S.now("vector"))],
                   inc=1, force_self=True)
            S.emit("vector", f_y2_b2, waits=[("vector", S.now("vector"))],
                   inc=1, force_self=True)
            S.emit("vector", f_y2_c, waits=[("vector", S.now("vector"))],
                   inc=1, force_self=True)
            S.emit("vector", lambda e: e.tensor_copy(
                out=tk["v2bf"][:], in_=tk["sxw2"][:]),
                waits=[("vector", S.now("vector")),
                       ("tensor", c_v2T_t.get(i - 1, 0))],
                inc=1, force_self=True)
            S.emit("vector", lambda e: e.reciprocal(
                out=tk["c2"][:], in_=tk["sxw2"][:]),
                waits=[("vector", S.now("vector"))], inc=1, force_self=True)
            c_y2 = S.now("vector")
            cv = S.emit("tensor", lambda e: e.transpose(
                out=ps_tp[0:1, 0:128], in_=tk["v2bf"][:], identity=idsb[:]),
                waits=[("vector", c_y2)], inc=1)
            c_v2T_t[i] = cv["tensor"]
            S.emit("vector", lambda e: e.tensor_copy(out=v2row[:],
                                                     in_=ps_tp[0:1, 0:128]),
                   waits=[("tensor", S.now("tensor")),
                          ("tensor", c_mm2q3.get(i - 1, 0))], inc=1)
            c_v2r = S.now("vector")

            c_mm2_q = {}
            c_qhtp = {}
            c_qhev_prev = {}
            for qq in range(4):
                def f_rne2(q_):
                    def f(e):
                        r = None
                        for hh in range(2):
                            sl = slice(q_ * 1024 + hh * 512,
                                       q_ * 1024 + hh * 512 + 512)
                            dl = slice(hh * 512, hh * 512 + 512)
                            e.tensor_scalar(out=fA[:], in0=H1[:, sl],
                                            scalar1=tk["s2"][:], scalar2=m1b[:],
                                            op0=ALU.mult, op1=ALU.add)
                            r = e.tensor_scalar(out=qh_sb[:, dl], in0=fA[:],
                                                scalar1=c3b[:], scalar2=m2kb[:],
                                                op0=ALU.add, op1=ALU.subtract)
                        return r
                    return f
                S.emit("vector", f_rne2(qq),
                       waits=[("vector", c_v2r),
                              ("tensor", c_qhtp.get(
                                  qq - 1, c_qh3_t.get(i - 1, 0)))],
                       inc=1, force_self=(qq == 0))
                c_qh = S.now("vector")
                qhTb = qhT0 if qq % 2 == 0 else qhT1

                def f_qhT(e):
                    r = None
                    for dc in range(8):
                        r = e.transpose(
                            out=ps_tp[:, dc * 128:(dc + 1) * 128],
                            in_=qh_sb[:, dc * 128:(dc + 1) * 128],
                            identity=idsb[:])
                    return r
                cv = S.emit("tensor", f_qhT,
                            waits=[("vector", c_qh),
                                   ("scalar", c_qhev_prev.get(qq - 1, 0)),
                                   ("tensor", c_mm2_q.get(qq - 2, 0))], inc=1)
                c_qhtp[qq] = cv["tensor"]
                if qq == 3:
                    c_qh3_t[i] = cv["tensor"]
                cv = S.emit("scalar", (lambda d: lambda e: e.activation(
                    out=d[:], in_=ps_tp[:], func=AF.Identity,
                    bias=zerob[:], scale=onesb[:]))(qhTb),
                    waits=[("tensor", S.now("tensor")),
                           ("tensor", c_mm2_q.get(
                               qq - 2, c_mm2q3.get(i - 1, 0)))], inc=1)
                c_qhev = cv["scalar"]
                c_qhev_prev[qq] = c_qhev
                if qq == 3:
                    c_qhev3[i] = c_qhev

                def f_mm2(q_, d):
                    def f(e):
                        r = None
                        for n in range(2):
                            o = ps_mm2[:, n * 512:(n + 1) * 512]
                            for c in range(8):
                                r = e.matmul(out=o,
                                             lhsT=d[:, c * 128:(c + 1) * 128],
                                             rhs=qw2T[:, q_ * 8 + c,
                                                      n * 512:(n + 1) * 512],
                                             start=(q_ == 0 and c == 0),
                                             stop=False)
                            if q_ == 3:
                                r = e.matmul(out=o, lhsT=v2row[:],
                                             rhs=b2row[0:1, n * 512:
                                                       (n + 1) * 512],
                                             start=False, stop=True)
                        return r
                    return f
                mm2w = [("scalar", c_qhev), ("vector", c_v2r)]
                if qq == 0:
                    mm2w += [("vector",
                              c_fin_t.get(i - 1, c_Msb if i == 0 else 0))]
                S.emit("tensor", f_mm2(qq, qhTb), waits=mm2w, inc=1)
                c_mm2_q[qq] = S.now("tensor")
                if qq == 3:
                    c_mm2q3[i] = c_mm2_q[qq]

        if NT_A2 and A2STOP == 'T':
            emit_tail(NT_A2 - 1)
        c_all_sync = S.now("sync")

        if DEBUG:
            _pre_ok = {"d_sw1b", "d_sw2b", "d_kb", "d_omkb", "d_m1b", "d_m2b",
                       "d_red1", "d_ttmp1", "d_part1",
                       "d_qw1c", "d_qw2c", "d_qw1full", "d_xn", "d_x",
                       "d_partials", "d_ttmp", "d_redsc"}
            dbg_specs = [
                ("d_sw1b", sw1b, [128, 1], F32),
                ("d_partials", partials, [128, 32], F32),
                ("d_ttmp", None, [128, 1], F32),
                ("d_redsc", redsc, [1, 4], F32),
                ("d_red1", dbg_red1, [1, 4], F32),
                ("d_xn0", dbg_xn0, [128, 8], BF16),
                ("d_sc0", dbg_sc0, [128, 8], F32),
                ("d_bn0", dbg_bn0, [128, 12], F32),
                ("d_ttmp1", dbg_ttmp1, [128, 2], F32),
                ("d_part1", dbg_part1, [128, 8], F32),
                ("d_sw2b", sw2b, [128, 1], F32),
                ("d_kb", kb, [128, 1], F32),
                ("d_m1b", m1b, [128, 1], F32),
                ("d_m2b", m2b, [128, 1], F32),
                ("d_qw1full", None, [128, 8, 4096], BF16),
                ("d_omkb", omkb, [128, 1], F32),
                ("d_msb", Msb, [128, 2, 1024], BF16),
                ("d_bnbuf", bnbuf, [128, 60], F32),
                ("d_mv", mv, [128, 2], F32),
                ("d_xn", xn_t, [128, D], BF16),
                ("d_qx", qx, [128, D], BF16),
                ("d_u", u_sb, [128, H], BF16),
                ("d_h1", H1, [128, H], BF16),
                ("d_qxT", qxT, [128, D], BF16),
                ("d_qw1c", None, [128, 1024], BF16),
                ("d_qw2c", None, [128, 1024], BF16),
                ("d_x", x_t, [128, D], F32),
                ("d_tk", None, [128, 16], F32),
                ("d_v1row", v1row, [1, 128], BF16),
                ("d_tht", None, [128, 256], BF16),
            ]
            wv = [("sync", S.now("sync")), ("vector", S.now("vector")),
                  ("scalar", S.now("scalar")), ("tensor", S.now("tensor")),
                  ("gpsimd", S.now("gpsimd"))]
            if PHASES == 'pre':
                dbg_specs = [t for t in dbg_specs if t[0] in _pre_ok]
            elif PHASES == 'a1':
                dbg_specs = [t for t in dbg_specs if t[0] in
                             (_pre_ok | {"d_msb", "d_tht", "d_x", "d_tk", "d_xn0", "d_sc0", "d_bn0"})]
            for nm, buf, shp, dt in dbg_specs:
                hh = nc.dram_tensor(nm, shp, dt, kind="ExternalOutput").ap()
                if nm == "d_qw1c":
                    src = qw1T[:, 0, 0:1024]
                elif nm == "d_qw2c":
                    src = qw2T[:, 0, 0:1024]
                elif nm == "d_ttmp":
                    src = tk["ttmp"][:]
                elif nm == "d_qw1full":
                    src = qw1T[:]
                elif nm == "d_tht":
                    src = tht1[:]
                elif nm == "d_tk":
                    src = None
                else:
                    src = buf[:]
                if nm == "d_tk":
                    names = ["rstd", "negmr", "rv", "nm", "amax", "s1",
                             "sxw", "c1", "mean2", "var2", "rstd2", "negmr2",
                             "amax2", "s2"]
                    dtk = sbt("dtk", [128, 16], F32)
                    def f_pack(e):
                        r = None
                        for j, n2 in enumerate(names):
                            r = e.tensor_copy(out=dtk[:, j:j + 1], in_=tk[n2][:])
                        e.tensor_copy(out=dtk[:, 14:16], in_=mv[:])
                        return r
                    S.emit("vector", f_pack, waits=wv, inc=1)
                    S.emit("sync", (lambda h2: lambda e: e.dma_start(
                        out=h2, in_=dtk[:]))(hh),
                        waits=[("vector", S.now("vector"))], inc=16)
                else:
                    S.emit("sync", (lambda h2, s2: lambda e: e.dma_start(
                        out=h2, in_=s2))(hh, src), waits=wv, inc=16)
            c_all_sync = S.now("sync")

        sems = {}
        for en in S.ENGINES:
            sems[en] = ctx.enter_context(nc.semaphore("sem_" + en))

        with nc.Block() as block:
            def make_body(engine_name):
                def body(eng):
                    watermark = {e: 0 for e in S.ENGINES}
                    for (en, fn, waits, inc) in S.ops:
                        if en != engine_name:
                            continue
                        for (wen, wval) in waits:
                            if wval > watermark[wen]:
                                eng.wait_ge(sems[wen], wval)
                                watermark[wen] = wval
                        r = fn(eng)
                        if inc:
                            assert r is not None, \
                                f"inc op on {engine_name} returned None"
                            r.then_inc(sems[en], inc)
                    if engine_name == "sync":
                        eng.wait_ge(sems["sync"], c_all_sync)
                return body
            block.sync(make_body("sync"))
            block.gpsimd(make_body("gpsimd"))
            block.vector(make_body("vector"))
            block.scalar(make_body("scalar"))
            block.tensor(make_body("tensor"))

    return nc


_NC_CACHE = None


def kernel(L0_attn_x, L0_attn_W_theta, L0_attn_W_phi, L0_attn_W_g,
           L0_ffn_ln1_g, L0_ffn_ln1_b, L0_ffn_w1, L0_ffn_b1,
           L0_ffn_ln2_g, L0_ffn_ln2_b, L0_ffn_w2, L0_ffn_b2):
    global _NC_CACHE
    f32c = lambda a: np.ascontiguousarray(np.asarray(a, np.float32))
    x = f32c(L0_attn_x)
    wt, wp, wg = f32c(L0_attn_W_theta), f32c(L0_attn_W_phi), f32c(L0_attn_W_g)
    w1, w2 = f32c(L0_ffn_w1), f32c(L0_ffn_w2)
    ln1g = np.ascontiguousarray(np.broadcast_to(f32c(L0_ffn_ln1_g), (128, D)))
    ln1b = np.ascontiguousarray(np.broadcast_to(f32c(L0_ffn_ln1_b), (128, D)))
    ln2g = np.ascontiguousarray(np.broadcast_to(f32c(L0_ffn_ln2_g), (128, H)))
    ln2b = np.ascontiguousarray(np.broadcast_to(f32c(L0_ffn_ln2_b), (128, H)))
    b1r = f32c(L0_ffn_b1).reshape(1, H)
    b2r = f32c(L0_ffn_b2).reshape(1, D)
    ident = np.eye(128, dtype=ml_dtypes.bfloat16)

    if _NC_CACHE is None:
        _NC_CACHE = build_nc()
    nc = _NC_CACHE

    in_maps = []
    for b in range(B):
        in_maps.append({
            "x": x[b], "wt": wt, "wp": wp, "wg": wg, "w1": w1, "w2": w2,
            "ln1g": ln1g, "ln1b": ln1b, "ln2g": ln2g, "ln2b": ln2b,
            "b1r": b1r, "b2r": b2r, "ident": ident,
        })
    res = run_bass_kernel_spmd(nc, in_maps, core_ids=list(range(B)))
    if DEBUG:
        return res
    return np.stack([np.asarray(r["out"], np.float32) for r in res.results],
                    axis=0)


# version 11 (final)
# speedup vs baseline: 1.0510x; 1.0510x over previous
import contextlib
import numpy as np
import ml_dtypes
import concourse.bass as bass
import concourse.mybir as mybir
from concourse.bass_utils import run_bass_kernel_spmd

F32 = mybir.dt.float32
BF16 = mybir.dt.bfloat16
AF = mybir.ActivationFunctionType
ALU = mybir.AluOpType
AX = mybir.AxisListType

B, T, D, LDIM, H = 8, 4096, 1024, 256, 4096
NT = T // 128
NDC = D // 128
NHC = H // 128
M1 = float(np.float32(1.5 * 2.0 ** 23))
M2 = float(1.5 * 2.0 ** 52)
EPS = 1e-5
ATT_SCALE = 1.0 / 32.0

COMPUTE_ENGINES = ("vector", "scalar", "tensor")
DEBUG = False
PHASES = 'full'
DETECT_RACES = True
A1_PE = True


class Sched:
    ENGINES = ("sync", "gpsimd", "vector", "scalar", "tensor")

    def __init__(self):
        self.ops = []
        self.cnt = {e: 0 for e in self.ENGINES}

    def emit(self, engine, fn, waits=(), inc=0, force_self=False):
        w = []
        for (en, val) in waits:
            if val is None or val <= 0:
                continue
            if en == engine and engine in COMPUTE_ENGINES and not force_self:
                continue
            w.append((en, val))
        self.ops.append((engine, fn, tuple(w), inc))
        if inc:
            self.cnt[engine] += inc
        return dict(self.cnt)

    def now(self, engine):
        return self.cnt[engine]


def build_nc():
    nc = bass.Bass("TRN2", target_bir_lowering=False,
                   dynamic_dma_scratch_size=16384,
                   detect_race_conditions=DETECT_RACES)

    x_h = nc.dram_tensor("x", [T, D], F32, kind="ExternalInput").ap()
    wt_h = nc.dram_tensor("wt", [LDIM, D], F32, kind="ExternalInput").ap()
    wp_h = nc.dram_tensor("wp", [LDIM, D], F32, kind="ExternalInput").ap()
    wg_h = nc.dram_tensor("wg", [D, D], F32, kind="ExternalInput").ap()
    w1_h = nc.dram_tensor("w1", [H, D], F32, kind="ExternalInput").ap()
    w2_h = nc.dram_tensor("w2", [D, H], F32, kind="ExternalInput").ap()
    ln1g_h = nc.dram_tensor("ln1g", [128, D], F32, kind="ExternalInput").ap()
    ln1b_h = nc.dram_tensor("ln1b", [128, D], F32, kind="ExternalInput").ap()
    ln2g_h = nc.dram_tensor("ln2g", [128, H], F32, kind="ExternalInput").ap()
    ln2b_h = nc.dram_tensor("ln2b", [128, H], F32, kind="ExternalInput").ap()
    b1_h = nc.dram_tensor("b1r", [1, H], F32, kind="ExternalInput").ap()
    b2_h = nc.dram_tensor("b2r", [1, D], F32, kind="ExternalInput").ap()
    id_h = nc.dram_tensor("ident", [128, 128], BF16, kind="ExternalInput").ap()
    out_h = nc.dram_tensor("out", [T, D], F32, kind="ExternalOutput").ap()
    xnscr = nc.dram_tensor("xnscr", [T, D], BF16, kind="Internal").ap()
    thscr = nc.dram_tensor("thscr", [2, 128, T], BF16, kind="Internal").ap()
    redscr = nc.dram_tensor("redscr", [128, 1], F32, kind="Internal").ap()

    S = Sched()
    _orig_emit = S.emit
    def _emit(engine, fn, waits=(), inc=0, force_self=False):
        if engine == "tensor" and not A1_PE and getattr(_emit, "_in_a1", False):
            return _orig_emit(engine, lambda e: None, waits, 0, force_self) or dict(S.cnt)
        return _orig_emit(engine, fn, waits, inc, force_self)


    with contextlib.ExitStack() as ctx:
        sbt = lambda nm, shp, dt: ctx.enter_context(nc.sbuf_tensor(nm, shp, dt))
        pst = lambda nm, shp, dt: ctx.enter_context(nc.psum_tensor(nm, shp, dt))

        ps_mm1 = pst("ps_mm1", [128, 2048], F32)
        ps_mm2 = pst("ps_mm2", [128, 1024], F32)
        ps_aux = pst("ps_aux", [128, 512], F32)
        ps_tp = pst("ps_tp", [128, 1024], BF16)

        onesb = sbt("onesb", [128, 1], F32)
        zerob = sbt("zerob", [128, 1], F32)
        m1b = sbt("m1b", [128, 1], F32)
        m2b = sbt("m2b", [128, 1], F32)
        kb = sbt("kb", [128, 1], F32)
        omkb = sbt("omkb", [128, 1], F32)
        m2kb = sbt("m2kb", [128, 1], F32)
        c3b = sbt("c3b", [128, 1], F32)
        onesrowb = sbt("onesrowb", [1, 128], BF16)
        ttmpb = sbt("ttmpb", [128, 1], BF16)
        redsc = sbt("redsc", [1, 4], F32)
        if DEBUG:
            dbg_red1 = sbt("dbg_red1", [1, 4], F32)
            dbg_xn0 = sbt("dbg_xn0", [128, 8], BF16)
            dbg_sc0 = sbt("dbg_sc0", [128, 8], F32)
            dbg_bn0 = sbt("dbg_bn0", [128, 12], F32)
            dbg_ttmp1 = sbt("dbg_ttmp1", [128, 2], F32)
            dbg_part1 = sbt("dbg_part1", [128, 8], F32)
        qw1T = sbt("qw1T", [128, NDC, H], BF16)
        qw2T = sbt("qw2T", [128, NHC, D], BF16)
        ln2gB = sbt("ln2gB", [128, H], BF16)
        ln2bB = sbt("ln2bB", [128, H], BF16)
        b1row = sbt("b1row", [1, H], BF16)
        b2row = sbt("b2row", [1, D], BF16)
        idsb = sbt("idsb", [128, 128], BF16)
        sw1b = sbt("sw1b", [128, 1], F32)
        sw2b = sbt("sw2b", [128, 1], F32)
        sw127b = sbt("sw127b", [128, 1], F32)
        sw2127b = sbt("sw2127b", [128, 1], F32)
        Msb = sbt("Msb", [128, 2, 1024], BF16)
        x_t = sbt("x_t", [128, D], F32)
        out_sb = sbt("out_sb", [128, D], F32)
        xn_t = sbt("xn_t", [128, D], BF16)
        fA = sbt("fA", [128, 512], F32)
        fB = sbt("fB", [128, 512], F32)
        bA = sbt("bA", [128, 512], BF16)
        bB = sbt("bB", [128, 512], BF16)
        bnbuf = sbt("bnbuf", [128, 48], F32)
        mv = sbt("mv", [128, 2], F32)
        partials = sbt("partials", [128, 32], F32)
        v1row = sbt("v1row", [1, 128], BF16)
        v2row = sbt("v2row", [1, 128], BF16)
        tk = {}
        for nm in ("rv", "nm", "rstd", "negmr", "amax", "s1", "sxw", "c1",
                   "v1bf", "mean2", "e2", "msq", "var2", "rv2", "nm2",
                   "rstd2", "negmr2", "amax2", "s2", "sxw2", "c2", "v2bf",
                   "ttmp"):
            tk[nm] = sbt("tk_" + nm, [128, 1],
                         BF16 if nm in ("v1bf", "v2bf") else F32)

        a1stack = ctx.enter_context(contextlib.ExitStack())
        sba = lambda nm, shp, dt: a1stack.enter_context(nc.sbuf_tensor(nm, shp, dt))
        xnT = sba("xnT", [128, D], BF16)
        WtT = sba("WtT", [128, NDC, LDIM], BF16)
        WpT = sba("WpT", [128, NDC, LDIM], BF16)
        WgT = sba("WgT", [128, NDC, D], BF16)
        phi_sb = sba("phi_sb", [128, LDIM], BF16)
        g_sb = sba("g_sb", [128, D], BF16)
        thst = sba("thst", [128, 256], BF16)

        stage_use = {0: [], 1: []}

        for buf, val in ((onesb, 1.0), (zerob, 0.0), (m1b, M1), (m2b, M2),
                         (onesrowb, 1.0), (redsc, 0.0)) + (
                             ((dbg_sc0, 0.0),) if DEBUG else ()):
            S.emit("vector",
                   (lambda b2, v2: lambda e: e.memset(b2[:], v2))(buf, val),
                   inc=1)
        S.emit("vector", lambda e: e.engine_nop() if False else
               e.tensor_copy(out=kb[:], in_=onesb[:]),
               waits=[("vector", S.now("vector"))], inc=1, force_self=True)
        S.emit("vector", lambda e: e.tensor_scalar(
            out=kb[:], in0=kb[:], scalar1=m2b[:], scalar2=m2b[:],
            op0=ALU.add, op1=ALU.subtract),
            waits=[("vector", S.now("vector"))], inc=1, force_self=True)
        S.emit("vector", lambda e: e.tensor_scalar(
            out=omkb[:], in0=kb[:], scalar1=1.0, scalar2=None, op0=ALU.is_lt),
            waits=[("vector", S.now("vector"))], inc=1, force_self=True)
        S.emit("vector", lambda e: e.tensor_scalar(
            out=m2kb[:], in0=kb[:], scalar1=m2b[:], scalar2=None, op0=ALU.mult),
            waits=[("vector", S.now("vector"))], inc=1, force_self=True)
        S.emit("vector", lambda e: e.tensor_scalar(
            out=c3b[:], in0=m2kb[:], scalar1=m1b[:], scalar2=None,
            op0=ALU.subtract),
            waits=[("vector", S.now("vector"))], inc=1, force_self=True)
        c_omk = S.now("vector")

        S.emit("scalar", lambda e: e.preload_act_table(AF.Sqrt), inc=1)
        S.emit("sync", lambda e: e.dma_start(out=idsb[:], in_=id_h), inc=16)
        c_ident = S.now("sync")
        for dst, src in ((ln2gB, ln2g_h),
                         (ln2bB, ln2b_h), (b1row, b1_h), (b2row, b2_h)):
            S.emit("gpsimd",
                   (lambda d, s: lambda e: e.dma_start(out=d[:], in_=s))(dst, src),
                   inc=16)
        c_smallg = S.now("gpsimd")

        def prep_quant_weight(srcs, swb, dsts):
            nch = len(srcs)
            for hc in range(nch):
                stage = x_t if hc % 2 == 0 else out_sb
                skey = hc % 2
                wv = S.emit("sync", (lambda s, src: lambda e: e.dma_start(
                    out=s[:], in_=src))(stage, srcs[hc]),
                    waits=list(stage_use[skey]), inc=16)
                stage_use[skey] = []
                cv = S.emit("scalar", (lambda s, h: lambda e: e.activation(
                    out=xn_t[:], in_=s[:], func=AF.Abs,
                    bias=zerob[:], scale=onesb[:]))(stage, hc),
                    waits=[("sync", wv["sync"]), ("vector", c_omk),
                           ("vector", S.now("vector"))], inc=1)
                stage_use[skey].append(("scalar", cv["scalar"]))
                S.emit("vector", (lambda h: lambda e: e.reduce_sum(
                    out=partials[:, h:h + 1], in_=xn_t[:], axis=AX.X))(hc),
                    waits=[("scalar", cv["scalar"])], inc=1)
                stage_use[skey].append(("vector", S.now("vector")))
            S.emit("vector", lambda e: e.reduce_sum(
                out=tk["ttmp"][:], in_=partials[:], axis=AX.X),
                waits=[("vector", S.now("vector"))], inc=1, force_self=True)
            S.emit("vector", lambda e: e.tensor_copy(
                out=ttmpb[:], in_=tk["ttmp"][:]),
                waits=[("vector", S.now("vector"))], inc=1, force_self=True)
            S.emit("vector", lambda e: e.tensor_copy(
                out=xn_t[:, 0:1], in_=ttmpb[:]),
                waits=[("vector", S.now("vector"))], inc=1, force_self=True)
            S.emit("tensor", lambda e: e.transpose(
                out=ps_tp[:, 0:128], in_=xn_t[:, 0:128], identity=idsb[:]),
                waits=[("vector", S.now("vector")), ("sync", c_ident)], inc=1)
            S.emit("vector", lambda e: e.reduce_sum(
                out=redsc[0:1, 0:1], in_=ps_tp[0:1, 0:128], axis=AX.X),
                waits=[("tensor", S.now("tensor"))], inc=1)
            S.emit("vector", lambda e: e.tensor_scalar(
                out=redsc[0:1, 1:2], in0=redsc[0:1, 0:1],
                scalar1=1.0 / float(32 * 128 * 1024), scalar2=1e-5,
                op0=ALU.mult, op1=ALU.max),
                waits=[("vector", S.now("vector"))], inc=1, force_self=True)
            S.emit("vector", lambda e: e.reciprocal(
                out=redsc[0:1, 2:3], in_=redsc[0:1, 1:2]),
                waits=[("vector", S.now("vector"))], inc=1, force_self=True)
            S.emit("vector", lambda e: e.tensor_scalar(
                out=xn_t[0:1, 0:128], in0=onesrowb[:], scalar1=redsc[0:1, 2:3],
                scalar2=None, op0=ALU.mult),
                waits=[("vector", S.now("vector"))], inc=1, force_self=True)
            S.emit("tensor", lambda e: e.transpose(
                out=ps_tp[:, 0:128], in_=xn_t[:, 0:128], identity=idsb[:]),
                waits=[("vector", S.now("vector"))], inc=1)
            S.emit("vector", (lambda w: lambda e: e.tensor_copy(
                out=w[:], in_=ps_tp[:, 0:1]))(swb),
                waits=[("tensor", S.now("tensor"))], inc=1)
            if DEBUG and swb is sw1b:
                def f_snap(e):
                    e.tensor_copy(out=dbg_red1[:], in_=redsc[:])
                    e.tensor_copy(out=dbg_ttmp1[:, 0:1], in_=tk["ttmp"][:])
                    e.tensor_copy(out=dbg_ttmp1[:, 1:2], in_=ps_tp[:, 0:1])
                    return e.tensor_copy(out=dbg_part1[:], in_=partials[:, 0:8])
                S.emit("vector", f_snap,
                       waits=[("vector", S.now("vector"))], inc=1,
                       force_self=True)
            c_swb = S.now("vector")

            for hc in range(nch):
                stage = x_t if hc % 2 == 0 else out_sb
                skey = hc % 2
                wv = S.emit("sync", (lambda s, src: lambda e: e.dma_start(
                    out=s[:], in_=src))(stage, srcs[hc]),
                    waits=list(stage_use[skey]) + [("vector", c_swb)], inc=16)
                stage_use[skey] = []

                def f_q(s, w):
                    def f(e):
                        r = None
                        for half in range(2):
                            sl = slice(half * 512, half * 512 + 512)
                            e.tensor_scalar(out=fA[:], in0=s[:, sl],
                                            scalar1=w[:], scalar2=m1b[:],
                                            op0=ALU.mult, op1=ALU.add)
                            e.tensor_scalar(out=fB[:], in0=fA[:],
                                            scalar1=c3b[:], scalar2=m2kb[:],
                                            op0=ALU.add, op1=ALU.subtract)
                            r = e.tensor_scalar(out=xn_t[:, sl], in0=fB[:],
                                                scalar1=-1.0, scalar2=1.0,
                                                op0=ALU.max, op1=ALU.min)
                        return r
                    return f
                cv = S.emit("vector", f_q(stage, swb),
                            waits=[("sync", wv["sync"]),
                                   ("vector", S.now("vector"))],
                            inc=1, force_self=True)
                stage_use[skey].append(("vector", cv["vector"]))
                c_q = S.now("vector")

                def f_tp(e):
                    r = None
                    for dc in range(8):
                        r = e.transpose(out=ps_tp[:, dc * 128:(dc + 1) * 128],
                                        in_=xn_t[:, dc * 128:(dc + 1) * 128],
                                        identity=idsb[:])
                    return r
                S.emit("tensor", f_tp,
                       waits=[("vector", c_q), ("sync", c_ident)], inc=1)
                S.emit("vector", (lambda dst: lambda e: e.tensor_copy(
                    out=dst, in_=ps_tp[:]))(dsts[hc]),
                    waits=[("tensor", S.now("tensor"))], inc=1)
            return S.now("vector")

        w1srcs = [w1_h[hc * 128:(hc + 1) * 128, :] for hc in range(32)]
        w1dsts = [qw1T[:, 0:NDC, hc * 128:(hc + 1) * 128] for hc in range(32)]
        prep_quant_weight(w1srcs, sw1b, w1dsts)
        w2srcs, w2dsts = [], []
        for rc in range(8):
            for cg in range(4):
                w2srcs.append(w2_h[rc * 128:(rc + 1) * 128,
                                   cg * 1024:(cg + 1) * 1024])
                w2dsts.append(qw2T[:, cg * 8:cg * 8 + 8, rc * 128:(rc + 1) * 128])
        c_wq = prep_quant_weight(w2srcs, sw2b, w2dsts)
        def f_sw127(e):
            e.tensor_scalar(out=sw127b[:], in0=sw1b[:], scalar1=127.0,
                            scalar2=None, op0=ALU.mult)
            return e.tensor_scalar(out=sw2127b[:], in0=sw2b[:], scalar1=127.0,
                                   scalar2=None, op0=ALU.mult)
        S.emit("vector", f_sw127, waits=[("vector", S.now("vector"))],
               inc=1, force_self=True)
        c_wq = S.now("vector")

        c_xnt_use = []

        def prep_attn_weight(w_hbm, nlc, dstT):
            nonlocal c_xnt_use
            for lc in range(nlc):
                wv = S.emit("gpsimd", (lambda l, w: lambda e: e.dma_start(
                    out=xn_t[:], in_=w[l * 128:(l + 1) * 128, :]))(lc, w_hbm),
                    waits=[("vector", S.now("vector"))], inc=16)

                def f_tp(e):
                    r = None
                    for dc in range(NDC):
                        r = e.transpose(out=ps_tp[:, dc * 128:(dc + 1) * 128],
                                        in_=xn_t[:, dc * 128:(dc + 1) * 128],
                                        identity=idsb[:])
                    return r
                S.emit("tensor", f_tp,
                       waits=[("gpsimd", wv["gpsimd"]), ("sync", c_ident)],
                       inc=1)
                S.emit("vector", (lambda l, d: lambda e: e.tensor_copy(
                    out=d[:, 0:NDC, l * 128:(l + 1) * 128],
                    in_=ps_tp[:]))(lc, dstT),
                    waits=[("tensor", S.now("tensor"))], inc=1)
            return S.now("vector")

        prep_attn_weight(wt_h, 2, WtT)
        prep_attn_weight(wp_h, 2, WpT)
        c_wattn = prep_attn_weight(wg_h, 8, WgT)

        c_xapply = {}
        c_xnw = {}
        c_thw = {}
        c_xnT_tp = {}
        NT_A1 = 0 if PHASES == 'pre' else NT
        for i in range(NT_A1):
            xwaits = [("scalar", c_xapply.get(i - 1, 0))]
            if i == 0:
                xwaits += [("vector", c_wattn)]
            wv_x = S.emit("sync", (lambda ii: lambda e: e.dma_start(
                out=x_t[:], in_=x_h[ii * 128:(ii + 1) * 128, :]))(i),
                waits=xwaits, inc=16)

            def f_bn_a(e):
                e.bn_stats(out=bnbuf[:, 0:6], in_=x_t[:, 0:512])
                return e.bn_stats(out=bnbuf[:, 6:12], in_=x_t[:, 512:1024])
            def f_bn_b(e):
                return e.bn_aggr(out=mv[:], in_=bnbuf[:, 0:12])
            def f_bn_c(e):
                e.tensor_scalar(out=tk["ttmp"][:], in0=mv[:, 1:2], scalar1=EPS,
                                scalar2=None, op0=ALU.add)
                return e.tensor_scalar(out=tk["nm"][:], in0=mv[:, 0:1],
                                       scalar1=-1.0, scalar2=None, op0=ALU.mult)
            bnw = [("sync", wv_x["sync"])]
            if i == 0:
                bnw += [("vector", c_wattn)]
            S.emit("vector", f_bn_a, waits=bnw, inc=1)
            S.emit("vector", f_bn_b, waits=[("vector", S.now("vector"))],
                   inc=1, force_self=True)
            S.emit("vector", f_bn_c, waits=[("vector", S.now("vector"))],
                   inc=1, force_self=True)
            S.emit("vector", lambda e: e.reciprocal(
                out=tk["rv"][:], in_=tk["ttmp"][:]),
                waits=[("vector", S.now("vector"))], inc=1, force_self=True)
            c_stats = S.now("vector")
            S.emit("scalar", lambda e: e.activation(
                out=tk["rstd"][:], in_=tk["rv"][:], func=AF.Sqrt,
                bias=zerob[:], scale=onesb[:]),
                waits=[("vector", c_stats)], inc=1)
            S.emit("scalar", lambda e: e.activation(
                out=tk["negmr"][:], in_=tk["nm"][:], func=AF.Identity,
                bias=zerob[:], scale=tk["rstd"][:]),
                waits=[("scalar", S.now("scalar"))], inc=1, force_self=True)
            S.emit("scalar", lambda e: e.activation(
                out=xn_t[:], in_=x_t[:], func=AF.Identity,
                bias=tk["negmr"][:], scale=tk["rstd"][:]),
                waits=[("scalar", S.now("scalar")),
                       ("gpsimd", c_xnw.get(i - 1, c_smallg)),
                       ("tensor", c_xnT_tp.get(i - 1, 0))],
                inc=1, force_self=True)
            c_xapply[i] = S.now("scalar")
            if DEBUG and i == 0:
                def f_snap0(e):
                    e.tensor_copy(out=dbg_xn0[:], in_=xn_t[:, 0:8])
                    e.tensor_copy(out=dbg_sc0[:, 0:1], in_=tk["rstd"][:])
                    e.tensor_copy(out=dbg_sc0[:, 1:2], in_=tk["negmr"][:])
                    e.tensor_copy(out=dbg_sc0[:, 2:3], in_=tk["rv"][:])
                    e.tensor_copy(out=dbg_sc0[:, 3:4], in_=tk["nm"][:])
                    e.tensor_copy(out=dbg_sc0[:, 4:5], in_=tk["ttmp"][:])
                    e.tensor_copy(out=dbg_sc0[:, 5:7], in_=mv[:])
                    return e.tensor_copy(out=dbg_bn0[:], in_=bnbuf[:, 0:12])
                S.emit("vector", f_snap0,
                       waits=[("scalar", c_xapply[i])], inc=1)
            wv = S.emit("gpsimd", (lambda ii: lambda e: e.dma_start(
                out=xnscr[ii * 128:(ii + 1) * 128, :], in_=xn_t[:]))(i),
                waits=[("scalar", c_xapply[i])], inc=16)
            c_xnw[i] = wv["gpsimd"]

            def f_xnT(e):
                r = None
                for dc in range(NDC):
                    r = e.transpose(out=ps_tp[:, dc * 128:(dc + 1) * 128],
                                    in_=xn_t[:, dc * 128:(dc + 1) * 128],
                                    identity=idsb[:])
                return r
            cv = S.emit("tensor", f_xnT,
                        waits=[("scalar", c_xapply[i]),
                               ("vector", S.now("vector"))], inc=1)
            c_xnT_tp[i] = cv["tensor"]
            S.emit("vector", lambda e: e.tensor_copy(out=xnT[:], in_=ps_tp[:]),
                   waits=[("tensor", S.now("tensor"))], inc=1)
            c_xnT = S.now("vector")

            def f_phi(e):
                r = None
                for c in range(NDC):
                    r = e.matmul(out=ps_aux[:, 0:LDIM],
                                 lhsT=xnT[:, c * 128:(c + 1) * 128],
                                 rhs=WpT[:, c, :],
                                 start=(c == 0), stop=(c == NDC - 1))
                return r
            S.emit("tensor", f_phi, waits=[("vector", c_xnT)], inc=1)
            c_phimm = S.now("tensor")

            def f_g(e):
                r = None
                for c in range(NDC):
                    for n in range(2):
                        r = e.matmul(out=ps_mm2[:, n * 512:(n + 1) * 512],
                                     lhsT=xnT[:, c * 128:(c + 1) * 128],
                                     rhs=WgT[:, c, n * 512:(n + 1) * 512],
                                     start=(c == 0), stop=(c == NDC - 1))
                return r
            S.emit("tensor", f_g, inc=1)
            c_gmm = S.now("tensor")
            S.emit("vector", lambda e: e.tensor_copy(out=phi_sb[:],
                                                     in_=ps_aux[:, 0:LDIM]),
                   waits=[("tensor", c_phimm)], inc=1)
            c_phiev = S.now("vector")
            S.emit("vector", lambda e: e.tensor_copy(out=g_sb[:], in_=ps_mm2[:]),
                   waits=[("tensor", c_gmm)], inc=1)
            c_gev = S.now("vector")

            for lc in range(2):
                def f_th(l):
                    def f(e):
                        r = None
                        for c in range(NDC):
                            r = e.matmul(out=ps_aux[:, 0:128],
                                         lhsT=WtT[:, c, l * 128:(l + 1) * 128],
                                         rhs=xnT[:, c * 128:(c + 1) * 128],
                                         start=(c == 0), stop=(c == NDC - 1))
                        return r
                    return f
                S.emit("tensor", f_th(lc),
                       waits=[("vector",
                               c_phiev if lc == 0 else S.now("vector"))], inc=1)
                S.emit("vector", (lambda l: lambda e: e.tensor_copy(
                    out=thst[:, l * 128:(l + 1) * 128],
                    in_=ps_aux[:, 0:128]))(lc),
                    waits=[("tensor", S.now("tensor")),
                           ("gpsimd", c_thw.get(i - 1, c_smallg) if lc == 0
                            else 0)], inc=1)
            c_thev = S.now("vector")

            S.emit("gpsimd", (lambda ii: lambda e: e.dma_start(
                out=thscr[0, :, ii * 128:(ii + 1) * 128],
                in_=thst[:, 0:128]))(i),
                waits=[("vector", c_thev)], inc=16)
            wv = S.emit("gpsimd", (lambda ii: lambda e: e.dma_start(
                out=thscr[1, :, ii * 128:(ii + 1) * 128],
                in_=thst[:, 128:256]))(i),
                waits=[("vector", c_thev)], inc=16)
            c_thw[i] = wv["gpsimd"]

            def f_M(ii):
                def f(e):
                    r = None
                    for lc in range(2):
                        for n in range(2):
                            o = ps_mm1[:, lc * 1024 + n * 512:
                                       lc * 1024 + n * 512 + 512]
                            r = e.matmul(out=o,
                                         lhsT=phi_sb[:, lc * 128:(lc + 1) * 128],
                                         rhs=g_sb[:, n * 512:(n + 1) * 512],
                                         start=(ii == 0), stop=(ii == NT - 1))
                    return r
                return f
            S.emit("tensor", f_M(i), waits=[("vector", c_gev)], inc=1)
        c_Mdone = S.now("tensor")
        c_A1_gp = S.now("gpsimd")

        def f_Mev(e):
            e.tensor_scalar(out=Msb[:, 0, :], in0=ps_mm1[:, 0:1024],
                            scalar1=ATT_SCALE, scalar2=None, op0=ALU.mult)
            return e.tensor_scalar(out=Msb[:, 1, :], in0=ps_mm1[:, 1024:2048],
                                   scalar1=ATT_SCALE, scalar2=None, op0=ALU.mult)
        if PHASES != 'pre':
            S.emit("vector", f_Mev, waits=[("tensor", c_Mdone)], inc=1)
        c_Msb = S.now("vector")
        c_A1_scalar = S.now("scalar")

        a1stack.close()
        qx = sbt("qx", [128, D], BF16)
        qxT = sbt("qxT", [128, D], BF16)
        u_sb = sbt("u_sb", [128, H], BF16)
        H1 = sbt("H1", [128, H], BF16)
        qh_sb = sbt("qh_sb", [128, 1024], BF16)
        qhT0 = sbt("qhT0", [128, 1024], BF16)
        qhT1 = sbt("qhT1", [128, 1024], BF16)
        tht0 = sbt("tht0", [128, 256], BF16)
        tht1 = sbt("tht1", [128, 256], BF16)
        ln1gB = sbt("ln1gB", [128, D], BF16)
        ln1bB = sbt("ln1bB", [128, D], BF16)
        for dst, src in ((ln1gB, ln1g_h), (ln1bB, ln1b_h)):
            S.emit("gpsimd",
                   (lambda d, s: lambda e: e.dma_start(out=d[:], in_=s))(dst, src),
                   waits=[("tensor", c_Mdone)], inc=16)

        NT_A2 = NT if PHASES == 'full' else 0
        c_gelu_q = {}
        c_zn1_t, c_o1_t, c_fin_t, c_att_t = {}, {}, {}, {}
        c_outdma, c_xdma, c_thdma = {}, {}, {}
        c_qh3_t, c_mm1q3, c_mm2q3, c_rne2q0 = {}, {}, {}, {}
        c_v1T_t, c_v2T_t, c_qxTtp = {}, {}, {}

        def f_att(d):
            def f(e):
                r = None
                for lc in range(2):
                    for n in range(2):
                        r = e.matmul(out=ps_mm2[:, n * 512:(n + 1) * 512],
                                     lhsT=d[:, lc * 128:(lc + 1) * 128],
                                     rhs=Msb[:, lc, n * 512:(n + 1) * 512],
                                     start=(lc == 0), stop=(lc == 1))
                return r
            return f

        def emit_tail(j):
            S.emit("vector", lambda e: e.scalar_tensor_tensor(
                out=out_sb[:], in0=ps_mm2[:], scalar=tk["c2"][:], in1=x_t[:],
                op0=ALU.mult, op1=ALU.add),
                waits=[("tensor", c_mm2q3[j]), ("sync", c_xdma[j]),
                       ("sync", c_outdma.get(j - 1, 0)),
                       ("vector", S.now("vector"))], inc=1, force_self=True)
            c_o1_t[j] = S.now("vector")
            tht_j = tht0 if j % 2 == 0 else tht1
            S.emit("tensor", f_att(tht_j),
                   waits=[("vector", max(c_o1_t[j], c_Msb)),
                          ("gpsimd", c_thdma[j])], inc=1)
            c_att_t[j] = S.now("tensor")
            S.emit("vector", lambda e: e.scalar_tensor_tensor(
                out=out_sb[:], in0=ps_mm2[:], scalar=1.0, in1=out_sb[:],
                op0=ALU.mult, op1=ALU.add),
                waits=[("tensor", c_att_t[j])], inc=1)
            c_fin_t[j] = S.now("vector")
            S.emit("sync", (lambda jj: lambda e: e.dma_start(
                out=out_h[jj * 128:(jj + 1) * 128, :], in_=out_sb[:]))(j),
                waits=[("vector", c_fin_t[j])], inc=16)
            c_outdma[j] = S.now("sync")

        def emit_xn_dma(j, extra):
            wv = S.emit("gpsimd", (lambda jj: lambda e: e.dma_start(
                out=xn_t[:], in_=xnscr[jj * 128:(jj + 1) * 128, :]))(j),
                waits=[("gpsimd", c_A1_gp)] + extra, inc=16)
            return wv["gpsimd"]

        def emit_tht_dma(j, extra):
            d = tht0 if j % 2 == 0 else tht1
            S.emit("gpsimd", (lambda jj, dd: lambda e: e.dma_start(
                out=dd[:, 0:128],
                in_=thscr[0, :, jj * 128:(jj + 1) * 128]))(j, d),
                waits=[("gpsimd", c_A1_gp)] + extra, inc=16)
            wv = S.emit("gpsimd", (lambda jj, dd: lambda e: e.dma_start(
                out=dd[:, 128:256],
                in_=thscr[1, :, jj * 128:(jj + 1) * 128]))(j, d),
                waits=[], inc=16)
            c_thdma[j] = wv["gpsimd"]

        c_xndma = {}
        if NT_A2:
            c_xndma[0] = emit_xn_dma(0, [("tensor", c_Mdone),
                                         ("scalar", c_A1_scalar),
                                         ("vector", c_Msb)])
            emit_tht_dma(0, [("tensor", c_Mdone)])

        for i in range(NT_A2):
            tht = tht0 if i % 2 == 0 else tht1

            def f_bn1_a(e):
                e.bn_stats(out=bnbuf[:, 0:6], in_=xn_t[:, 0:512])
                return e.bn_stats(out=bnbuf[:, 6:12], in_=xn_t[:, 512:1024])
            def f_bn1_b(e):
                return e.bn_aggr(out=mv[:], in_=bnbuf[:, 0:12])
            def f_bn1_c(e):
                e.tensor_scalar(out=tk["ttmp"][:], in0=mv[:, 1:2], scalar1=EPS,
                                scalar2=None, op0=ALU.add)
                return e.tensor_scalar(out=tk["nm"][:], in0=mv[:, 0:1],
                                       scalar1=-1.0, scalar2=None, op0=ALU.mult)
            S.emit("vector", f_bn1_a, waits=[("gpsimd", c_xndma[i])], inc=1)
            S.emit("vector", f_bn1_b, waits=[("vector", S.now("vector"))],
                   inc=1, force_self=True)
            S.emit("vector", f_bn1_c, waits=[("vector", S.now("vector"))],
                   inc=1, force_self=True)
            S.emit("vector", lambda e: e.reciprocal(
                out=tk["rv"][:], in_=tk["ttmp"][:]),
                waits=[("vector", S.now("vector"))], inc=1, force_self=True)
            c_st1 = S.now("vector")
            S.emit("scalar", lambda e: e.activation(
                out=tk["rstd"][:], in_=tk["rv"][:], func=AF.Sqrt,
                bias=zerob[:], scale=onesb[:]),
                waits=[("vector", c_st1)], inc=1)
            S.emit("scalar", lambda e: e.activation(
                out=tk["negmr"][:], in_=tk["nm"][:], func=AF.Identity,
                bias=zerob[:], scale=tk["rstd"][:]),
                waits=[("scalar", S.now("scalar"))], inc=1, force_self=True)
            S.emit("scalar", lambda e: e.activation(
                out=H1[:, 0:D], in_=xn_t[:], func=AF.Identity,
                bias=tk["negmr"][:], scale=tk["rstd"][:]),
                waits=[("scalar", S.now("scalar")),
                       ("vector", c_rne2q0.get(i - 1, 0))],
                inc=1, force_self=True)
            c_zn1_t[i] = S.now("scalar")

            def f_y1_a(e):
                for hh in range(2):
                    sl = slice(hh * 512, hh * 512 + 512)
                    e.tensor_tensor(out=bA[:], in0=H1[:, sl], in1=ln1gB[:, sl],
                                    op=ALU.mult)
                    e.tensor_tensor(out=H1[:, sl], in0=bA[:], in1=ln1bB[:, sl],
                                    op=ALU.add)
                return e.tensor_reduce(out=tk["amax"][:], in_=H1[:, 0:D],
                                       axis=AX.X, op=ALU.max,
                                       apply_absolute_value=True)
            def f_y1_b(e):
                return e.tensor_scalar(out=tk["amax"][:], in0=tk["amax"][:],
                                       scalar1=1e-5, scalar2=None, op0=ALU.max)
            def f_y1_b2(e):
                return e.reciprocal(out=tk["s1"][:], in_=tk["amax"][:])
            def f_y1_c(e):
                e.tensor_tensor(out=tk["sxw"][:], in0=tk["s1"][:],
                                in1=sw127b[:], op=ALU.mult)
                return e.tensor_scalar(out=tk["s1"][:], in0=tk["s1"][:],
                                       scalar1=127.0, scalar2=None,
                                       op0=ALU.mult)
            S.emit("vector", f_y1_a, waits=[("scalar", c_zn1_t[i])], inc=1)
            S.emit("vector", f_y1_b, waits=[("vector", S.now("vector"))],
                   inc=1, force_self=True)
            S.emit("vector", f_y1_b2, waits=[("vector", S.now("vector"))],
                   inc=1, force_self=True)
            S.emit("vector", f_y1_c, waits=[("vector", S.now("vector"))],
                   inc=1, force_self=True)
            S.emit("vector", lambda e: e.tensor_copy(
                out=tk["v1bf"][:], in_=tk["sxw"][:]),
                waits=[("vector", S.now("vector")),
                       ("tensor", c_v1T_t.get(i - 1, 0))],
                inc=1, force_self=True)
            S.emit("vector", lambda e: e.reciprocal(
                out=tk["c1"][:], in_=tk["sxw"][:]),
                waits=[("vector", S.now("vector"))], inc=1, force_self=True)

            def f_rne1(e):
                r = None
                for hh in range(2):
                    sl = slice(hh * 512, hh * 512 + 512)
                    e.tensor_scalar(out=fA[:], in0=H1[:, sl],
                                    scalar1=tk["s1"][:], scalar2=m1b[:],
                                    op0=ALU.mult, op1=ALU.add)
                    r = e.tensor_scalar(out=qx[:, sl], in0=fA[:],
                                        scalar1=c3b[:], scalar2=m2kb[:],
                                        op0=ALU.add, op1=ALU.subtract)
                return r
            S.emit("vector", f_rne1,
                   waits=[("vector", S.now("vector")),
                          ("tensor", c_qxTtp.get(i - 1, 0))],
                   inc=1, force_self=True)
            c_qx = S.now("vector")

            def f_qxT(e):
                r = None
                for dc in range(NDC):
                    r = e.transpose(out=ps_tp[:, dc * 128:(dc + 1) * 128],
                                    in_=qx[:, dc * 128:(dc + 1) * 128],
                                    identity=idsb[:])
                return r
            cv = S.emit("tensor", f_qxT, waits=[("vector", c_qx)], inc=1)
            c_qxTtp[i] = cv["tensor"]
            S.emit("vector", lambda e: e.tensor_copy(out=qxT[:], in_=ps_tp[:]),
                   waits=[("tensor", S.now("tensor")),
                          ("tensor", c_mm1q3.get(i - 1, 0))], inc=1)
            c_qxT = S.now("vector")
            cv = S.emit("tensor", lambda e: e.transpose(
                out=ps_tp[0:1, 0:128], in_=tk["v1bf"][:], identity=idsb[:]),
                waits=[("vector", c_qxT)], inc=1)
            c_v1T_t[i] = cv["tensor"]
            S.emit("vector", lambda e: e.tensor_copy(out=v1row[:],
                                                     in_=ps_tp[0:1, 0:128]),
                   waits=[("tensor", S.now("tensor")),
                          ("tensor", c_mm1q3.get(i - 1, 0))], inc=1)
            c_v1r = S.now("vector")

            for q in range(4):
                def f_mm1(qq):
                    def f(e):
                        r = None
                        base = (qq % 2) * 1024
                        for n in range(2):
                            o = ps_mm1[:, base + n * 512: base + n * 512 + 512]
                            for c in range(NDC):
                                e.matmul(out=o,
                                         lhsT=qxT[:, c * 128:(c + 1) * 128],
                                         rhs=qw1T[:, c, qq * 1024 + n * 512:
                                                  qq * 1024 + n * 512 + 512],
                                         start=(c == 0), stop=False)
                            r = e.matmul(out=o, lhsT=v1row[:],
                                         rhs=b1row[0:1, qq * 1024 + n * 512:
                                                   qq * 1024 + n * 512 + 512],
                                         start=False, stop=True)
                        return r
                    return f
                S.emit("tensor", f_mm1(q),
                       waits=[("vector", max(c_v1r, c_wq)),
                              ("scalar", c_gelu_q.get(
                                  (i, q - 2), c_gelu_q.get((i - 1, q + 2), 0)))],
                       inc=1)
                c_mm1q = S.now("tensor")
                if q == 3:
                    c_mm1q3[i] = c_mm1q

                def f_gelu(qq):
                    def f(e):
                        base = (qq % 2) * 1024
                        return e.activation(
                            out=u_sb[:, qq * 1024:(qq + 1) * 1024],
                            in_=ps_mm1[:, base:base + 1024],
                            func=AF.Gelu, bias=zerob[:], scale=tk["c1"][:])
                    return f
                if q == 0:
                    S.emit("scalar", lambda e: e.preload_act_table(AF.Gelu),
                           inc=1)
                cv = S.emit("scalar", f_gelu(q),
                            waits=[("tensor", c_mm1q), ("vector", c_qx)], inc=1)
                c_gelu_q[(i, q)] = cv["scalar"]

            if i > 0:
                emit_tail(i - 1)

            if i + 1 < NT_A2:
                c_xndma[i + 1] = emit_xn_dma(i + 1,
                                             [("scalar", c_zn1_t[i])])
                emit_tht_dma(i + 1, [("tensor", c_att_t.get(i - 1, 0))])
            xw = [("vector", c_o1_t.get(i - 1, 0))]
            if i == 0:
                xw += [("scalar", c_A1_scalar), ("vector", c_Msb)]
            wv = S.emit("sync", (lambda ii: lambda e: e.dma_start(
                out=x_t[:], in_=x_h[ii * 128:(ii + 1) * 128, :]))(i),
                waits=xw, inc=16)
            c_xdma[i] = wv["sync"]

            def f_bn2(qq):
                def f(e):
                    e.bn_stats(out=bnbuf[:, qq * 12:qq * 12 + 6],
                               in_=u_sb[:, qq * 1024:qq * 1024 + 512])
                    return e.bn_stats(
                        out=bnbuf[:, qq * 12 + 6:qq * 12 + 12],
                        in_=u_sb[:, qq * 1024 + 512:qq * 1024 + 1024])
                return f
            for q in range(4):
                S.emit("vector", f_bn2(q),
                       waits=[("scalar", c_gelu_q[(i, q)])], inc=1)
            S.emit("vector", lambda e: e.bn_aggr(out=mv[:], in_=bnbuf[:]),
                   waits=[("vector", S.now("vector"))], inc=1, force_self=True)
            def f_st2_b(e):
                e.tensor_copy(out=tk["mean2"][:], in_=mv[:, 0:1])
                e.tensor_copy(out=tk["var2"][:], in_=mv[:, 1:2])
                e.tensor_scalar(out=tk["ttmp"][:], in0=mv[:, 1:2],
                                scalar1=EPS, scalar2=None, op0=ALU.add)
                return e.tensor_scalar(out=tk["nm2"][:], in0=mv[:, 0:1],
                                       scalar1=-1.0, scalar2=None, op0=ALU.mult)
            S.emit("vector", f_st2_b, waits=[("vector", S.now("vector"))],
                   inc=1, force_self=True)
            S.emit("vector", lambda e: e.reciprocal(
                out=tk["rv2"][:], in_=tk["ttmp"][:]),
                waits=[("vector", S.now("vector"))], inc=1, force_self=True)
            c_st2 = S.now("vector")
            S.emit("scalar", lambda e: e.preload_act_table(AF.Sqrt), inc=1)
            S.emit("scalar", lambda e: e.activation(
                out=tk["rstd2"][:], in_=tk["rv2"][:], func=AF.Sqrt,
                bias=zerob[:], scale=onesb[:]),
                waits=[("vector", c_st2)], inc=1)
            S.emit("scalar", lambda e: e.activation(
                out=tk["negmr2"][:], in_=tk["nm2"][:], func=AF.Identity,
                bias=zerob[:], scale=tk["rstd2"][:]),
                waits=[("scalar", S.now("scalar"))], inc=1, force_self=True)
            S.emit("scalar", lambda e: e.activation(
                out=H1[:], in_=u_sb[:], func=AF.Identity,
                bias=tk["negmr2"][:], scale=tk["rstd2"][:]),
                waits=[("scalar", S.now("scalar"))], inc=1, force_self=True)
            c_zn2 = S.now("scalar")

            def f_y2_a(e):
                for hh in range(8):
                    sl = slice(hh * 512, hh * 512 + 512)
                    e.tensor_tensor(out=bA[:], in0=H1[:, sl], in1=ln2gB[:, sl],
                                    op=ALU.mult)
                    e.tensor_tensor(out=H1[:, sl], in0=bA[:], in1=ln2bB[:, sl],
                                    op=ALU.add)
                return e.tensor_reduce(out=tk["amax2"][:], in_=H1[:],
                                       axis=AX.X, op=ALU.max,
                                       apply_absolute_value=True)
            def f_y2_b(e):
                return e.tensor_scalar(out=tk["amax2"][:], in0=tk["amax2"][:],
                                       scalar1=1e-5, scalar2=None, op0=ALU.max)
            def f_y2_b2(e):
                return e.reciprocal(out=tk["s2"][:], in_=tk["amax2"][:])
            def f_y2_c(e):
                e.tensor_tensor(out=tk["sxw2"][:], in0=tk["s2"][:],
                                in1=sw2127b[:], op=ALU.mult)
                return e.tensor_scalar(out=tk["s2"][:], in0=tk["s2"][:],
                                       scalar1=127.0, scalar2=None,
                                       op0=ALU.mult)
            S.emit("vector", f_y2_a, waits=[("scalar", c_zn2)], inc=1)
            S.emit("vector", f_y2_b, waits=[("vector", S.now("vector"))],
                   inc=1, force_self=True)
            S.emit("vector", f_y2_b2, waits=[("vector", S.now("vector"))],
                   inc=1, force_self=True)
            S.emit("vector", f_y2_c, waits=[("vector", S.now("vector"))],
                   inc=1, force_self=True)
            S.emit("vector", lambda e: e.tensor_copy(
                out=tk["v2bf"][:], in_=tk["sxw2"][:]),
                waits=[("vector", S.now("vector")),
                       ("tensor", c_v2T_t.get(i - 1, 0))],
                inc=1, force_self=True)
            S.emit("vector", lambda e: e.reciprocal(
                out=tk["c2"][:], in_=tk["sxw2"][:]),
                waits=[("vector", S.now("vector"))], inc=1, force_self=True)
            c_y2 = S.now("vector")
            cv = S.emit("tensor", lambda e: e.transpose(
                out=ps_tp[0:1, 0:128], in_=tk["v2bf"][:], identity=idsb[:]),
                waits=[("vector", c_y2)], inc=1)
            c_v2T_t[i] = cv["tensor"]
            S.emit("vector", lambda e: e.tensor_copy(out=v2row[:],
                                                     in_=ps_tp[0:1, 0:128]),
                   waits=[("tensor", S.now("tensor")),
                          ("tensor", c_mm2q3.get(i - 1, 0))], inc=1)
            c_v2r = S.now("vector")

            c_mm2_q = {}
            c_qhtp = {}
            for qq in range(4):
                def f_rne2(q_):
                    def f(e):
                        r = None
                        for hh in range(2):
                            sl = slice(q_ * 1024 + hh * 512,
                                       q_ * 1024 + hh * 512 + 512)
                            dl = slice(hh * 512, hh * 512 + 512)
                            e.tensor_scalar(out=fA[:], in0=H1[:, sl],
                                            scalar1=tk["s2"][:], scalar2=m1b[:],
                                            op0=ALU.mult, op1=ALU.add)
                            r = e.tensor_scalar(out=qh_sb[:, dl], in0=fA[:],
                                                scalar1=c3b[:], scalar2=m2kb[:],
                                                op0=ALU.add, op1=ALU.subtract)
                        return r
                    return f
                S.emit("vector", f_rne2(qq),
                       waits=[("vector", c_v2r),
                              ("tensor", c_qhtp.get(
                                  qq - 1, c_qh3_t.get(i - 1, 0)))],
                       inc=1, force_self=(qq == 0))
                c_qh = S.now("vector")
                if qq == 0:
                    c_rne2q0[i] = c_qh
                qhTb = qhT0 if qq % 2 == 0 else qhT1

                def f_qhT(e):
                    r = None
                    for dc in range(8):
                        r = e.transpose(
                            out=ps_tp[:, dc * 128:(dc + 1) * 128],
                            in_=qh_sb[:, dc * 128:(dc + 1) * 128],
                            identity=idsb[:])
                    return r
                cv = S.emit("tensor", f_qhT,
                            waits=[("vector", c_qh),
                                   ("tensor", c_mm2_q.get(qq - 2, 0))], inc=1)
                c_qhtp[qq] = cv["tensor"]
                if qq == 3:
                    c_qh3_t[i] = cv["tensor"]
                S.emit("vector", (lambda d: lambda e: e.tensor_copy(
                    out=d[:], in_=ps_tp[:]))(qhTb),
                    waits=[("tensor", S.now("tensor")),
                           ("tensor", c_mm2_q.get(
                               qq - 2, c_mm2q3.get(i - 1, 0)))], inc=1)
                c_qhTev = S.now("vector")

                def f_mm2(q_, d):
                    def f(e):
                        r = None
                        for n in range(2):
                            o = ps_mm2[:, n * 512:(n + 1) * 512]
                            for c in range(8):
                                r = e.matmul(out=o,
                                             lhsT=d[:, c * 128:(c + 1) * 128],
                                             rhs=qw2T[:, q_ * 8 + c,
                                                      n * 512:(n + 1) * 512],
                                             start=(q_ == 0 and c == 0),
                                             stop=False)
                            if q_ == 3:
                                r = e.matmul(out=o, lhsT=v2row[:],
                                             rhs=b2row[0:1, n * 512:
                                                       (n + 1) * 512],
                                             start=False, stop=True)
                        return r
                    return f
                mm2w = [("vector", max(c_qhTev, c_v2r))]
                if qq == 0:
                    mm2w += [("vector",
                              c_fin_t.get(i - 1, c_Msb if i == 0 else 0))]
                S.emit("tensor", f_mm2(qq, qhTb), waits=mm2w, inc=1)
                c_mm2_q[qq] = S.now("tensor")
                if qq == 3:
                    c_mm2q3[i] = c_mm2_q[qq]

        if NT_A2:
            emit_tail(NT_A2 - 1)
        c_all_sync = S.now("sync")

        if DEBUG:
            _pre_ok = {"d_sw1b", "d_sw2b", "d_kb", "d_omkb", "d_m1b", "d_m2b",
                       "d_red1", "d_ttmp1", "d_part1",
                       "d_qw1c", "d_qw2c", "d_qw1full", "d_xn", "d_x",
                       "d_partials", "d_ttmp", "d_redsc"}
            dbg_specs = [
                ("d_sw1b", sw1b, [128, 1], F32),
                ("d_partials", partials, [128, 32], F32),
                ("d_ttmp", None, [128, 1], F32),
                ("d_redsc", redsc, [1, 4], F32),
                ("d_red1", dbg_red1, [1, 4], F32),
                ("d_xn0", dbg_xn0, [128, 8], BF16),
                ("d_sc0", dbg_sc0, [128, 8], F32),
                ("d_bn0", dbg_bn0, [128, 12], F32),
                ("d_ttmp1", dbg_ttmp1, [128, 2], F32),
                ("d_part1", dbg_part1, [128, 8], F32),
                ("d_sw2b", sw2b, [128, 1], F32),
                ("d_kb", kb, [128, 1], F32),
                ("d_m1b", m1b, [128, 1], F32),
                ("d_m2b", m2b, [128, 1], F32),
                ("d_qw1full", None, [128, 8, 4096], BF16),
                ("d_omkb", omkb, [128, 1], F32),
                ("d_msb", Msb, [128, 2, 1024], BF16),
                ("d_bnbuf", bnbuf, [128, 48], F32),
                ("d_mv", mv, [128, 2], F32),
                ("d_xn", xn_t, [128, D], BF16),
                ("d_qx", qx, [128, D], BF16),
                ("d_u", u_sb, [128, H], BF16),
                ("d_h1", H1, [128, H], BF16),
                ("d_qxT", qxT, [128, D], BF16),
                ("d_qw1c", None, [128, 1024], BF16),
                ("d_qw2c", None, [128, 1024], BF16),
                ("d_x", x_t, [128, D], F32),
                ("d_tk", None, [128, 16], F32),
                ("d_v1row", v1row, [1, 128], BF16),
                ("d_tht", None, [128, 256], BF16),
            ]
            wv = [("sync", S.now("sync")), ("vector", S.now("vector")),
                  ("scalar", S.now("scalar")), ("tensor", S.now("tensor")),
                  ("gpsimd", S.now("gpsimd"))]
            if PHASES == 'pre':
                dbg_specs = [t for t in dbg_specs if t[0] in _pre_ok]
            elif PHASES == 'a1':
                dbg_specs = [t for t in dbg_specs if t[0] in
                             (_pre_ok | {"d_msb", "d_tht", "d_x", "d_tk", "d_xn0", "d_sc0", "d_bn0"})]
            for nm, buf, shp, dt in dbg_specs:
                hh = nc.dram_tensor(nm, shp, dt, kind="ExternalOutput").ap()
                if nm == "d_qw1c":
                    src = qw1T[:, 0, 0:1024]
                elif nm == "d_qw2c":
                    src = qw2T[:, 0, 0:1024]
                elif nm == "d_ttmp":
                    src = tk["ttmp"][:]
                elif nm == "d_qw1full":
                    src = qw1T[:]
                elif nm == "d_tht":
                    src = tht1[:]
                elif nm == "d_tk":
                    src = None
                else:
                    src = buf[:]
                if nm == "d_tk":
                    names = ["rstd", "negmr", "rv", "nm", "amax", "s1",
                             "sxw", "c1", "mean2", "var2", "rstd2", "negmr2",
                             "amax2", "s2"]
                    dtk = sbt("dtk", [128, 16], F32)
                    def f_pack(e):
                        r = None
                        for j, n2 in enumerate(names):
                            r = e.tensor_copy(out=dtk[:, j:j + 1], in_=tk[n2][:])
                        e.tensor_copy(out=dtk[:, 14:16], in_=mv[:])
                        return r
                    S.emit("vector", f_pack, waits=wv, inc=1)
                    S.emit("sync", (lambda h2: lambda e: e.dma_start(
                        out=h2, in_=dtk[:]))(hh),
                        waits=[("vector", S.now("vector"))], inc=16)
                else:
                    S.emit("sync", (lambda h2, s2: lambda e: e.dma_start(
                        out=h2, in_=s2))(hh, src), waits=wv, inc=16)
            c_all_sync = S.now("sync")

        sems = {}
        for en in S.ENGINES:
            sems[en] = ctx.enter_context(nc.semaphore("sem_" + en))

        with nc.Block() as block:
            def make_body(engine_name):
                def body(eng):
                    watermark = {e: 0 for e in S.ENGINES}
                    for (en, fn, waits, inc) in S.ops:
                        if en != engine_name:
                            continue
                        for (wen, wval) in waits:
                            if wval > watermark[wen]:
                                eng.wait_ge(sems[wen], wval)
                                watermark[wen] = wval
                        r = fn(eng)
                        if inc:
                            assert r is not None, \
                                f"inc op on {engine_name} returned None"
                            r.then_inc(sems[en], inc)
                    if engine_name == "sync":
                        eng.wait_ge(sems["sync"], c_all_sync)
                return body
            block.sync(make_body("sync"))
            block.gpsimd(make_body("gpsimd"))
            block.vector(make_body("vector"))
            block.scalar(make_body("scalar"))
            block.tensor(make_body("tensor"))

    return nc


_NC_CACHE = None


def kernel(L0_attn_x, L0_attn_W_theta, L0_attn_W_phi, L0_attn_W_g,
           L0_ffn_ln1_g, L0_ffn_ln1_b, L0_ffn_w1, L0_ffn_b1,
           L0_ffn_ln2_g, L0_ffn_ln2_b, L0_ffn_w2, L0_ffn_b2):
    global _NC_CACHE
    f32c = lambda a: np.ascontiguousarray(np.asarray(a, np.float32))
    x = f32c(L0_attn_x)
    wt, wp, wg = f32c(L0_attn_W_theta), f32c(L0_attn_W_phi), f32c(L0_attn_W_g)
    w1, w2 = f32c(L0_ffn_w1), f32c(L0_ffn_w2)
    ln1g = np.ascontiguousarray(np.broadcast_to(f32c(L0_ffn_ln1_g), (128, D)))
    ln1b = np.ascontiguousarray(np.broadcast_to(f32c(L0_ffn_ln1_b), (128, D)))
    ln2g = np.ascontiguousarray(np.broadcast_to(f32c(L0_ffn_ln2_g), (128, H)))
    ln2b = np.ascontiguousarray(np.broadcast_to(f32c(L0_ffn_ln2_b), (128, H)))
    b1r = f32c(L0_ffn_b1).reshape(1, H)
    b2r = f32c(L0_ffn_b2).reshape(1, D)
    ident = np.eye(128, dtype=ml_dtypes.bfloat16)

    if _NC_CACHE is None:
        _NC_CACHE = build_nc()
    nc = _NC_CACHE

    in_maps = []
    for b in range(B):
        in_maps.append({
            "x": x[b], "wt": wt, "wp": wp, "wg": wg, "w1": w1, "w2": w2,
            "ln1g": ln1g, "ln1b": ln1b, "ln2g": ln2g, "ln2b": ln2b,
            "b1r": b1r, "b2r": b2r, "ident": ident,
        })
    res = run_bass_kernel_spmd(nc, in_maps, core_ids=list(range(B)))
    if DEBUG:
        return res
    return np.stack([np.asarray(r["out"], np.float32) for r in res.results],
                    axis=0)
